# version 02
# baseline (speedup 1.0000x reference)
import numpy as np
import ml_dtypes
import concourse.bass as bass
import concourse.mybir as mybir
from concourse.bass_utils import run_bass_kernel_spmd
from contextlib import ExitStack

F32 = mybir.dt.float32
BF16 = mybir.dt.bfloat16
I32 = mybir.dt.int32
AF = mybir.ActivationFunctionType
ALU = mybir.AluOpType
AX = mybir.AxisListType
BF = ml_dtypes.bfloat16

D = 1024
S = 2048
NH = 16
FF = 4096
NEG = -1.0e30
EPS = 1e-5

_STAGES = ("l0a", "l0f", "l1a", "l1f")
_FUSED = True
_DEBUG_TAPS = []
_TAP_RESULTS = None
_STAGE_LIMIT = None


class Prog:
    ENGINES = ("sync", "gpsimd", "scalar", "vector", "tensor")
    STEP = {"sync": 16, "gpsimd": 16, "scalar": 1, "vector": 1, "tensor": 1}

    def __init__(self):
        self.count = {e: 0 for e in self.ENGINES}
        self.items = {e: [] for e in self.ENGINES}
        self.last_write = {}
        self.readers = {}
        self.waited = {e: {} for e in self.ENGINES}
        self.prev_writes = {e: set() for e in self.ENGINES}

    def do(self, eng, fn, reads=(), writes=()):
        waits = []

        def need(pch, pcount):
            if pch == eng and eng not in ("sync", "gpsimd"):
                return
            if self.waited[eng].get(pch, 0) >= pcount:
                return
            self.waited[eng][pch] = pcount
            waits.append((pch, pcount))

        for b in reads:
            lw = self.last_write.get(b)
            if lw:
                need(*lw)
        for b in writes:
            lw = self.last_write.get(b)
            if lw:
                need(*lw)
            for r in self.readers.get(b, ()):
                need(*r)
        if eng in ("scalar", "vector", "gpsimd") and \
                (set(reads) | set(writes)) & self.prev_writes[eng]:
            waits.append((eng, self.count[eng]))
        self.count[eng] += 1
        ca = self.count[eng]
        for b in writes:
            self.last_write[b] = (eng, ca)
            self.readers[b] = []
        for b in reads:
            self.readers.setdefault(b, []).append((eng, ca))
        self.prev_writes[eng] = set(writes)
        self.items[eng].append((waits, fn))

    def barrier(self, eng, bufs):
        waits = []
        for b in bufs:
            lw = self.last_write.get(b)
            if lw and lw[0] != eng and self.waited[eng].get(lw[0], 0) < lw[1]:
                self.waited[eng][lw[0]] = lw[1]
                waits.append(lw)
        if waits:
            self.items[eng].append((waits, None))

    def emit(self, block, sems):
        def run(section, en):
            for waits, fn in self.items[en]:
                for (pch, pcount) in waits:
                    section.wait_ge(sems[pch], pcount * self.STEP[pch])
                if fn is not None:
                    fn(section).then_inc(sems[en], self.STEP[en])
        block.sync(lambda s: run(s, "sync"))
        block.gpsimd(lambda g: run(g, "gpsimd"))
        block.scalar(lambda s: run(s, "scalar"))
        block.vector(lambda v: run(v, "vector"))
        block.tensor(lambda t: run(t, "tensor"))


def pb(i):
    return f"PB{i}"


def pc(i):
    return f"PC{i}"


PBALL = tuple(pb(i) for i in range(4))
PCALL = tuple(pc(i) for i in range(4))


class Core:
    def __init__(self, nc, ctx):
        E = ctx.enter_context
        self.nc = nc
        self.P = Prog()
        self.X = [E(nc.sbuf_tensor(f"x{c}", [128, S], F32)) for c in range(8)]
        self.A = [E(nc.sbuf_tensor(f"A{c}", [128, S], BF16)) for c in range(8)]
        self.B = [E(nc.sbuf_tensor(f"B{c}", [128, S], BF16)) for c in range(8)]
        self.VT = [E(nc.sbuf_tensor(f"VT{c}", [128, 1040], BF16, is_transpose_dest=True))
                   for c in range(16)]
        self.POOL = [E(nc.sbuf_tensor(f"PL{c}", [128, S], BF16, is_transpose_dest=True))
                     for c in range(6)]
        self.SCR4 = E(nc.sbuf_tensor("SCR4", [128, 1024], F32))
        self.WF = self.SCR4
        self.FS = [self.SCR4[:, 0:512], self.SCR4[:, 512:1024]]
        self.IS = E(nc.sbuf_tensor("IS0", [128, 512], I32))
        self.bias = E(nc.sbuf_tensor("biasc", [128, 72], F32))
        self.row = [E(nc.sbuf_tensor(f"row{i}", [1, S], BF16)) for i in range(2)]
        self.crow = E(nc.sbuf_tensor("crow", [1, 64], F32))
        self.bc = [E(nc.sbuf_tensor(f"bc{i}", [128, 512], BF16)) for i in range(2)]
        self.ones = E(nc.sbuf_tensor("ones", [128, 1], F32))
        self.onesb = E(nc.sbuf_tensor("onesb", [128, 1], BF16))
        self.onesr = E(nc.sbuf_tensor("onesr", [1, 128], F32))
        self.onesrb = E(nc.sbuf_tensor("onesrb", [1, 128], BF16))
        self.mask = E(nc.sbuf_tensor("maskT", [128, 384], BF16))
        self.eye = E(nc.sbuf_tensor("eye", [128, 128], F32))
        self.sc384 = E(nc.sbuf_tensor("sc384", [128, 384], F32))
        self.ex = [E(nc.sbuf_tensor(f"ex{i}", [128, 384], BF16)) for i in range(3)]
        self.vsc = E(nc.sbuf_tensor("vsc", [128, 64], BF16))
        self.srow = E(nc.sbuf_tensor("srow", [128, 16], F32))
        self.vT = [E(nc.sbuf_tensor(f"vT{i}", [128, 16], BF16, is_transpose_dest=True))
                   for i in range(8)]
        self.fbT = E(nc.sbuf_tensor("fbT", [128, 16], BF16, is_transpose_dest=True))
        self.PB = E(nc.psum_tensor([128, S], F32))
        self.PC = E(nc.psum_tensor([128, S], F32))
        self.sems = {e: E(nc.semaphore(f"sem_{e}")) for e in Prog.ENGINES}

    def dma(self, dst, src, key, reads=(), q="sync"):
        self.P.do(q, lambda s: s.dma_start(out=dst, in_=src), reads=reads, writes=(key,))


def pool_slot(core, i):
    return core.POOL[1 + i // 2][:, 1024 * (i % 2):1024 * (i % 2 + 1)], f"PL{1 + i // 2}"


def load_nat_w(core, wdram, nslots, slot_fn):
    P, C = core.P, core
    for i in range(nslots):
        dst, dkey = slot_fn(i)
        P.do("sync", lambda s, i=i: s.dma_start(
            out=C.WF[:, 0:1024], in_=wdram[128 * i:128 * (i + 1), :]),
            reads=(), writes=("SCR4a", "SCR4b"))
        P.do("vector", lambda v, dst=dst: v.tensor_copy(dst, C.WF[:, 0:1024]),
             reads=("SCR4a", "SCR4b"), writes=(dkey,))


def load_bias_col(core, bdram, slot, n=8):
    P, C = core.P, core
    for i in range(n):
        P.do("sync", lambda s, i=i: s.dma_start(
            out=C.bias[:, slot + i:slot + i + 1], in_=bdram[128 * i:128 * (i + 1), :]),
            reads=(), writes=(f"bias{slot + i}",))


def proj_1024(core, wkey_fmt, out_fn, bias_slot):
    P, C = core.P, core
    for co in range(8):
        for w in range(4):
            for i in range(8):
                sl, skey = pool_slot(core, i)
                P.do("tensor", lambda t, co=co, w=w, i=i, sl=sl: t.matmul(
                    C.PB[:, 512 * w:512 * (w + 1)], sl[:, 128 * co:128 * (co + 1)],
                    C.A[i][:, 512 * w:512 * (w + 1)], start=(i == 0), stop=(i == 7)),
                    reads=(skey, f"A{i}"), writes=(pb(w),))
            out_fn(co, w)


def ln(core):
    P, C = core.P, core
    sq = C.POOL[0]
    for c in range(8):
        P.do("gpsimd", lambda g, c=c: g.tensor_copy(C.A[c][:], C.X[c][:]),
             reads=(f"X{c}",), writes=(f"A{c}",))
    for c in range(8):
        P.do("scalar", lambda s, c=c: s.activation(sq[:], C.A[c][:], AF.Square),
             reads=(f"A{c}",), writes=("PL0",))
        for w in range(4):
            P.do("tensor", lambda t, c=c, w=w: t.matmul(
                C.PB[0:1, 512 * w:512 * (w + 1)], C.onesb[:, 0:1],
                C.A[c][:, 512 * w:512 * (w + 1)], start=(c == 0), stop=(c == 7)),
                reads=(f"A{c}", "onesb"), writes=(pb(w),))
            P.do("tensor", lambda t, c=c, w=w: t.matmul(
                C.PC[0:1, 512 * w:512 * (w + 1)], C.onesb[:, 0:1],
                sq[:, 512 * w:512 * (w + 1)], start=(c == 0), stop=(c == 7)),
                reads=("PL0", "onesb"), writes=(pc(w),))
    P.do("vector", lambda v: v.tensor_single_scalar(C.row[0][:], C.PB[0:1, :], 1.0 / D, ALU.mult),
         reads=PBALL, writes=("row0",))
    P.do("vector", lambda v: v.tensor_single_scalar(C.row[1][:], C.PC[0:1, :], 1.0 / D, ALU.mult),
         reads=PCALL, writes=("row1",))
    P.do("scalar", lambda s: s.activation(C.PB[0:1, 0:S], C.row[0][:], AF.Square),
         reads=("row0",), writes=PBALL)
    P.do("vector", lambda v: v.tensor_tensor(C.row[1][:], C.row[1][:], C.PB[0:1, 0:S], ALU.subtract),
         reads=("row1",) + PBALL, writes=("row1",))
    P.do("vector", lambda v: v.tensor_single_scalar(C.row[1][:], C.row[1][:], EPS, ALU.add),
         reads=("row1",), writes=("row1",))
    P.do("scalar", lambda s: s.activation(C.row[1][:], C.row[1][:], AF.Sqrt),
         reads=("row1",), writes=("row1",))
    def _recip_r1(v):
        with C.nc.allow_low_precision(reason="bf16 rstd row"):
            return v.reciprocal(C.row[1][:], C.row[1][:])
    P.do("vector", _recip_r1, reads=("row1",), writes=("row1",))
    for w in range(4):
        P.do("tensor", lambda t, w=w: t.matmul(
            C.PB[:, 0:512], C.onesrb[0:1, :], C.row[0][0:1, 512 * w:512 * (w + 1)],
            start=True, stop=True), reads=("row0", "onesrb"), writes=(pb(0),))
        P.do("vector", lambda v: v.tensor_copy(C.bc[0][:], C.PB[:, 0:512]),
             reads=(pb(0),), writes=("bc0",))
        P.do("tensor", lambda t, w=w: t.matmul(
            C.PB[:, 0:512], C.onesrb[0:1, :], C.row[1][0:1, 512 * w:512 * (w + 1)],
            start=True, stop=True), reads=("row1", "onesrb"), writes=(pb(0),))
        P.do("vector", lambda v: v.tensor_copy(C.bc[1][:], C.PB[:, 0:512]),
             reads=(pb(0),), writes=("bc1",))
        for c in range(8):
            P.do("vector", lambda v, c=c, w=w: v.tensor_tensor(
                C.FS[0], C.X[c][:, 512 * w:512 * (w + 1)], C.bc[0][:], ALU.subtract),
                reads=(f"X{c}", "bc0"), writes=("SCR4a",))
            P.do("gpsimd", lambda g, c=c, w=w: g.tensor_tensor(
                C.A[c][:, 512 * w:512 * (w + 1)], C.FS[0], C.bc[1][:], ALU.mult),
                reads=("SCR4a", "bc1"), writes=(f"A{c}",))


def head_norm_bound(core, sq_ap, sq_key, hh, crow_slot):
    P, C = core.P, core
    keys = sq_key if isinstance(sq_key, tuple) else (sq_key,)
    for w in range(4):
        P.do("tensor", lambda t, hh=hh, w=w: t.matmul(
            C.PC[0:1, 512 * w:512 * (w + 1)], C.onesb[64 * hh:64 * (hh + 1), 0:1],
            sq_ap[64 * hh:64 * (hh + 1), 512 * w:512 * (w + 1)],
            start=True, stop=True), reads=keys + ("onesb",), writes=(pc(w),))
    P.do("vector", lambda v, crow_slot=crow_slot: v.tensor_reduce(
        C.crow[0:1, crow_slot:crow_slot + 1], C.PC[0:1, :], AX.X, ALU.max),
        reads=PCALL, writes=(f"crow{crow_slot}",))


def stage_l0_attn(core, W):
    P, C = core.P, core
    ln(core)
    if _STAGE_LIMIT == "ln":
        return
    load_nat_w(core, W["L0_attn_Wv"], 8, lambda i: pool_slot(core, i))
    load_bias_col(core, W["L0_attn_bv"], 16)

    def v_out(co, w):
        P.do("vector", lambda v, co=co, w=w: v.tensor_single_scalar(
            C.POOL[5][:, 512 * w:512 * (w + 1)], C.PB[:, 512 * w:512 * (w + 1)],
            C.bias[:, 16 + co:17 + co], ALU.add),
            reads=(pb(w), f"bias{16 + co}"), writes=("PL5a" if w < 2 else "PL5b",))
        if w == 3:
            P.do("sync", lambda s, co=co: s.dma_start(
                out=W["scrV"][128 * co:128 * (co + 1), :], in_=C.POOL[5][:, :]),
                reads=("PL5a", "PL5b"), writes=("SCRV",))
    proj_1024(core, None, v_out, 16)
    for tt in range(16):
        P.do("sync", lambda s, tt=tt: s.dma_start(
            out=C.VT[tt][:, 0:1024], in_=W["scrV"][:, 128 * tt:128 * (tt + 1)],
            transpose=True), reads=("SCRV",), writes=(f"VT{tt}",))
    for tt in range(16):
        P.do("vector", lambda v, tt=tt: v.tensor_copy(C.POOL[5][:, 0:1024], C.VT[tt][:, 0:1024]),
             reads=(f"VT{tt}",), writes=("PL5a",))
        P.do("vector", lambda v, tt=tt: v.tensor_copy(
            C.VT[tt][:].rearrange("p (h e) -> p h e", h=16)[:, :, 0:64],
            C.POOL[5][:, 0:1024].rearrange("p (h e) -> p h e", h=16)),
            reads=("PL5a",), writes=(f"VT{tt}",))
        P.do("vector", lambda v, tt=tt: v.memset(
            C.VT[tt][:].rearrange("p (h e) -> p h e", h=16)[:, :, 64:65], 1.0),
            reads=(), writes=(f"VT{tt}",))
    if _STAGE_LIMIT == "v":
        return
    load_nat_w(core, W["L0_attn_Wk"], 8, lambda i: pool_slot(core, i))
    load_bias_col(core, W["L0_attn_bk"], 8)

    def k_out(co, w):
        P.do("vector", lambda v, co=co, w=w: v.tensor_single_scalar(
            C.B[co][:, 512 * w:512 * (w + 1)], C.PB[:, 512 * w:512 * (w + 1)],
            C.bias[:, 8 + co:9 + co], ALU.add),
            reads=(pb(w), f"bias{8 + co}"), writes=(f"B{co}",))
    proj_1024(core, None, k_out, 8)
    sq = C.POOL[0]
    for co in range(8):
        P.do("scalar", lambda s, co=co: s.activation(sq[:], C.B[co][:], AF.Square),
             reads=(f"B{co}",), writes=("PL0",))
        for hh in range(2):
            head_norm_bound(core, sq[:], "PL0", hh, 2 * co + hh)
    if _STAGE_LIMIT == "k":
        return
    load_nat_w(core, W["L0_attn_Wq"], 8, lambda i: pool_slot(core, i))
    load_bias_col(core, W["L0_attn_bq"], 0)
    qc = C.POOL[0]
    for co in range(8):
        for w in range(4):
            for i in range(8):
                sl, skey = pool_slot(core, i)
                P.do("tensor", lambda t, co=co, w=w, i=i, sl=sl: t.matmul(
                    C.PB[:, 512 * w:512 * (w + 1)], sl[:, 128 * co:128 * (co + 1)],
                    C.A[i][:, 512 * w:512 * (w + 1)], start=(i == 0), stop=(i == 7)),
                    reads=(skey, f"A{i}"), writes=(pb(w),))
            P.do("vector", lambda v, co=co, w=w: v.tensor_single_scalar(
                qc[:, 512 * w:512 * (w + 1)], C.PB[:, 512 * w:512 * (w + 1)],
                C.bias[:, co:co + 1], ALU.add),
                reads=(pb(w), f"bias{co}"), writes=("PL0",))
        P.do("scalar", lambda s: s.activation(C.POOL[5][:, 0:S], qc[:], AF.Square),
             reads=("PL0",), writes=("PL5a", "PL5b"))
        for hh in range(2):
            h = 2 * co + hh
            head_norm_bound(core, C.POOL[5][:, 0:S], ("PL5a", "PL5b"), hh, 32 + h)
            P.do("tensor", lambda t, h=h: t.matmul(
                C.PC[:, 0:1], C.onesr[0:1, :], C.crow[0:1, 32 + h:33 + h],
                start=True, stop=True), reads=(f"crow{32 + h}", "onesr"), writes=(pc(0),))
            P.do("tensor", lambda t, h=h: t.matmul(
                C.PC[:, 512:513], C.onesr[0:1, :], C.crow[0:1, h:h + 1],
                start=True, stop=True), reads=(f"crow{h}", "onesr"), writes=(pc(1),))
            P.do("vector", lambda v: v.tensor_copy(C.srow[:, 0:1], C.PC[:, 0:1]),
                 reads=(pc(0),), writes=("srow0",))
            P.do("vector", lambda v: v.tensor_tensor(
                C.srow[:, 1:2], C.srow[:, 0:1], C.PC[:, 512:513], ALU.mult),
                reads=("srow0", pc(1)), writes=("srow1",))
            P.do("scalar", lambda s: s.activation(
                C.srow[:, 2:3], C.srow[:, 1:2], AF.Sqrt, scale=1.0 / 64.0),
                reads=("srow1",), writes=("srow2",))
            P.do("vector", lambda v, h=h: v.tensor_single_scalar(
                C.bias[:, 32 + h:33 + h], C.srow[:, 2:3], -1.0, ALU.mult),
                reads=("srow2",), writes=(f"bias{32 + h}",))
        for hh in range(2):
            h = 2 * co + hh
            hb = 64 * hh
            for kb in range(16):
                j0 = 0 if kb > 0 else 128
                j1 = 384 if kb < 15 else 256
                q0 = 128 * (kb - 1)
                sb = kb % 2
                P.do("tensor", lambda t, co=co, hb=hb, kb=kb, j0=j0, j1=j1, q0=q0, sb=sb: t.matmul(
                    C.PB[:, 512 * sb + j0:512 * sb + j1],
                    C.B[co][hb:hb + 64, 128 * kb:128 * (kb + 1)],
                    qc[hb:hb + 64, q0 + j0:q0 + j1], start=True, stop=True),
                    reads=(f"B{co}", "PL0"), writes=(pb(sb),))
                scb = C.sc384[:, 0:384] if sb == 0 else C.SCR4[:, 0:384]
                sck = "sc384" if sb == 0 else "SCR4a"
                P.do("vector", lambda v, j0=j0, j1=j1, sb=sb, scb=scb: v.tensor_tensor(
                    scb[:, j0:j1], C.PB[:, 512 * sb + j0:512 * sb + j1],
                    C.mask[:, j0:j1], ALU.add),
                    reads=(pb(sb), "mask"), writes=(sck,))
                P.do("scalar", lambda s, j0=j0, j1=j1, h=h, kb=kb, scb=scb: s.activation(
                    C.ex[kb % 3][:, j0:j1], scb[:, j0:j1], AF.Exp,
                    bias=C.bias[:, 32 + h:33 + h], scale=0.125),
                    reads=(sck, f"bias{32 + h}"), writes=(f"ex{kb % 3}",))
                regs = [kb - 1] if kb > 0 else []
                if kb == 15:
                    regs.append(15)
                for r in regs:
                    ks = [k2 for k2 in (r - 1, r, r + 1) if 0 <= k2 <= 15]
                    for k2 in ks:
                        j = r - (k2 - 1)
                        P.do("tensor", lambda t, r=r, k2=k2, j=j, h=h, ks=tuple(ks): t.matmul(
                            C.PC[0:65, 128 * r:128 * (r + 1)],
                            C.VT[k2][:, 65 * h:65 * h + 65],
                            C.ex[k2 % 3][:, 128 * j:128 * (j + 1)],
                            start=(k2 == ks[0]), stop=(k2 == ks[-1])),
                            reads=(f"VT{k2}", f"ex{k2 % 3}"), writes=(pc(r // 4),))
            def _recip_s(v):
                with C.nc.allow_low_precision(reason="bf16 softmax-sum recip"):
                    return v.reciprocal(C.row[0][:], C.PC[64:65, :])
            P.do("vector", _recip_s, reads=PCALL, writes=("row0",))
            if _STAGE_LIMIT == "sums0" and co == 0 and hh == 0:
                return
            for w in range(4):
                P.do("tensor", lambda t, w=w: t.matmul(
                    C.PB[0:64, 1024:1536], C.onesrb[0:1, 0:64],
                    C.row[0][0:1, 512 * w:512 * (w + 1)], start=True, stop=True),
                    reads=("row0", "onesrb"), writes=(pb(2),))
                P.do("vector", lambda v: v.tensor_copy(C.bc[0][0:64, :], C.PB[0:64, 1024:1536]),
                     reads=(pb(2),), writes=("bc0",))
                P.do("vector", lambda v, w=w, co=co, hb=hb: v.tensor_tensor(
                    C.B[co][hb:hb + 64, 512 * w:512 * (w + 1)],
                    C.PC[0:64, 512 * w:512 * (w + 1)], C.bc[0][0:64, :], ALU.mult),
                    reads=(pc(w), "bc0"), writes=(f"B{co}",))
            if _STAGE_LIMIT == "h0" and co == 0 and hh == 0:
                return
        if _STAGE_LIMIT == "q0" and co == 0:
            return
    if _STAGE_LIMIT == "heads":
        return
    def vt_slot(i):
        return C.VT[i][:, 0:1024], f"VT{i}"
    load_nat_w(core, W["L0_attn_Wo"], 8, vt_slot)
    load_bias_col(core, W["L0_attn_bo"], 24)
    for o in range(8):
        for w in range(4):
            bank = w
            for i in range(8):
                P.do("tensor", lambda t, o=o, w=w, i=i, bank=bank: t.matmul(
                    C.PB[:, 512 * bank:512 * (bank + 1)], C.VT[i][:, 128 * o:128 * (o + 1)],
                    C.B[i][:, 512 * w:512 * (w + 1)], start=(i == 0), stop=(i == 7)),
                    reads=(f"VT{i}", f"B{i}"), writes=(pb(bank),))
            P.do("vector", lambda v, o=o, w=w, bank=bank: v.scalar_tensor_tensor(
                C.X[o][:, 512 * w:512 * (w + 1)], C.PB[:, 512 * bank:512 * (bank + 1)],
                C.bias[:, 24 + o:25 + o], C.X[o][:, 512 * w:512 * (w + 1)],
                ALU.add, ALU.add),
                reads=(pb(bank), f"bias{24 + o}", f"X{o}"), writes=(f"X{o}",))


def bounce_1024(core, wdram, scr, dst_slot_fn):
    P, C = core.P, core
    for i in range(8):
        P.do("sync", lambda s, i=i: s.dma_start(
            out=C.WF[:, 0:1024], in_=wdram[128 * i:128 * (i + 1), :]),
            reads=(), writes=("SCR4a", "SCR4b"))
        P.do("vector", lambda v: v.tensor_copy(C.POOL[5][:, 0:1024], C.WF[:, 0:1024]),
             reads=("SCR4a", "SCR4b"), writes=("PL5a",))
        P.do("sync", lambda s, i=i: s.dma_start(
            out=scr[128 * i:128 * (i + 1), 0:1024], in_=C.POOL[5][:, 0:1024]),
            reads=("PL5a",), writes=("SCRW",))
    for j in range(8):
        dst, dkey = dst_slot_fn(j)
        P.do("sync", lambda s, j=j, dst=dst: s.dma_start(
            out=dst, in_=scr[0:1024, 128 * j:128 * (j + 1)], transpose=True),
            reads=("SCRW",), writes=(dkey,))


def stage_l1_attn(core, W):
    P, C = core.P, core
    ln(core)
    if _STAGE_LIMIT == "ln":
        return
    bounce_1024(core, W["L1_attn_v_w"], W["scrW"], lambda j: pool_slot(core, j))
    load_bias_col(core, W["L1_attn_v_b"], 16)

    def v_out(co, w):
        P.do("vector", lambda v, co=co, w=w: v.tensor_single_scalar(
            C.POOL[5][:, 512 * w:512 * (w + 1)], C.PB[:, 512 * w:512 * (w + 1)],
            C.bias[:, 16 + co:17 + co], ALU.add),
            reads=(pb(w), f"bias{16 + co}"), writes=("PL5a" if w < 2 else "PL5b",))
        if w == 3:
            P.do("sync", lambda s, co=co: s.dma_start(
                out=W["scrV"][128 * co:128 * (co + 1), :], in_=C.POOL[5][:, :]),
                reads=("PL5a", "PL5b"), writes=("SCRV",))
    proj_1024(core, None, v_out, 16)
    for tt in range(16):
        P.do("sync", lambda s, tt=tt: s.dma_start(
            out=C.VT[tt][:, 0:1024], in_=W["scrV"][:, 128 * tt:128 * (tt + 1)],
            transpose=True), reads=("SCRV",), writes=(f"VT{tt}",))
    if _STAGE_LIMIT == "v":
        return
    bounce_1024(core, W["L1_attn_k_w"], W["scrW"], lambda j: pool_slot(core, j))
    load_bias_col(core, W["L1_attn_k_b"], 8)

    def k_out(co, w):
        P.do("vector", lambda v, co=co, w=w: v.tensor_single_scalar(
            C.B[co][:, 512 * w:512 * (w + 1)], C.PB[:, 512 * w:512 * (w + 1)],
            C.bias[:, 8 + co:9 + co], ALU.add),
            reads=(pb(w), f"bias{8 + co}"), writes=(f"B{co}",))
    proj_1024(core, None, k_out, 8)
    if _STAGE_LIMIT == "k":
        return
    bounce_1024(core, W["L1_attn_q_w"], W["scrW"], lambda j: pool_slot(core, j))
    load_bias_col(core, W["L1_attn_q_b"], 0)
    qc = C.POOL[0]
    for co in range(8):
        for w in range(4):
            for i in range(8):
                sl, skey = pool_slot(core, i)
                P.do("tensor", lambda t, co=co, w=w, i=i, sl=sl: t.matmul(
                    C.PB[:, 512 * w:512 * (w + 1)], sl[:, 128 * co:128 * (co + 1)],
                    C.A[i][:, 512 * w:512 * (w + 1)], start=(i == 0), stop=(i == 7)),
                    reads=(skey, f"A{i}"), writes=(pb(w),))
            P.do("vector", lambda v, co=co, w=w: v.tensor_single_scalar(
                qc[:, 512 * w:512 * (w + 1)], C.PB[:, 512 * w:512 * (w + 1)],
                C.bias[:, co:co + 1], ALU.add),
                reads=(pb(w), f"bias{co}"), writes=("PL0",))
        for hh in range(2):
            h = 2 * co + hh
            hb = 64 * hh
            for kc in range(16):
                for w in range(4):
                    P.do("tensor", lambda t, co=co, hb=hb, kc=kc, w=w: t.matmul(
                        C.PB[:, 512 * w:512 * (w + 1)],
                        C.B[co][hb:hb + 64, 128 * kc:128 * (kc + 1)],
                        qc[hb:hb + 64, 512 * w:512 * (w + 1)], start=True, stop=True),
                        reads=(f"B{co}", "PL0"), writes=(pb(w),))
                P.do("vector", lambda v: v.tensor_reduce(
                    C.srow[:, 5:6], C.PB[:, 0:S], AX.X, ALU.max, negate=True),
                    reads=PBALL, writes=("srow5",))
                P.do("vector", lambda v: v.tensor_single_scalar(
                    C.srow[:, 6:7], C.srow[:, 5:6], 0.125, ALU.mult),
                    reads=("srow5",), writes=("srow6",))
                P.do("scalar", lambda s: s.activation(
                    C.POOL[5][:, 0:S], C.PB[:, 0:S], AF.Exp,
                    bias=C.srow[:, 6:7], scale=0.125,
                    accum_out=C.srow[:, 0:1]),
                    reads=PBALL + ("srow6",), writes=("PL5a", "PL5b", "srow0"))
                P.do("vector", lambda v: v.reciprocal(C.srow[:, 1:2], C.srow[:, 0:1]),
                     reads=("srow0",), writes=("srow1",))
                P.do("vector", lambda v, kc=kc, h=h: v.tensor_single_scalar(
                    C.vsc[:], C.VT[kc][:, 64 * h:64 * h + 64],
                    C.srow[:, 1:2], ALU.mult),
                    reads=(f"VT{kc}", "srow1"), writes=("vsc",))
                for w in range(4):
                    P.do("tensor", lambda t, kc=kc, w=w: t.matmul(
                        C.PC[0:64, 512 * w:512 * (w + 1)], C.vsc[:],
                        C.POOL[5][:, 512 * w:512 * (w + 1)],
                        start=(kc == 0), stop=(kc == 15)),
                        reads=("vsc", "PL5a" if w < 2 else "PL5b"), writes=(pc(w),))
            for w in range(4):
                P.do("vector", lambda v, co=co, hb=hb, w=w: v.tensor_copy(
                    C.B[co][hb:hb + 64, 512 * w:512 * (w + 1)],
                    C.PC[0:64, 512 * w:512 * (w + 1)]),
                    reads=(pc(w),), writes=(f"B{co}",))
            if _STAGE_LIMIT == "h0" and co == 0 and hh == 0:
                return
        if _STAGE_LIMIT == "q0" and co == 0:
            return
    if _STAGE_LIMIT == "heads":
        return
    def vt_slot(i):
        return C.VT[i][:, 0:1024], f"VT{i}"
    bounce_1024(core, W["L1_attn_l_w"], W["scrW"], vt_slot)
    load_bias_col(core, W["L1_attn_l_b"], 24)
    for o in range(8):
        for w in range(4):
            bank = w
            for i in range(8):
                P.do("tensor", lambda t, o=o, w=w, i=i, bank=bank: t.matmul(
                    C.PB[:, 512 * bank:512 * (bank + 1)], C.VT[i][:, 128 * o:128 * (o + 1)],
                    C.B[i][:, 512 * w:512 * (w + 1)], start=(i == 0), stop=(i == 7)),
                    reads=(f"VT{i}", f"B{i}"), writes=(pb(bank),))
            P.do("vector", lambda v, o=o, w=w, bank=bank: v.scalar_tensor_tensor(
                C.X[o][:, 512 * w:512 * (w + 1)], C.PB[:, 512 * bank:512 * (bank + 1)],
                C.bias[:, 24 + o:25 + o], C.X[o][:, 512 * w:512 * (w + 1)],
                ALU.add, ALU.add),
                reads=(pb(bank), f"bias{24 + o}", f"X{o}"), writes=(f"X{o}",))


def quant_rowtile(core, wdram, row0, width, inv_n, stage_tiles, out_store):
    P, C = core.P, core
    ns = width // 1024
    for s4 in range(ns):
        P.do("sync", lambda s, s4=s4: s.dma_start(
            out=C.WF[:, 0:1024], in_=wdram[row0:row0 + 128, 1024 * s4:1024 * (s4 + 1)]),
            reads=(), writes=("SCR4a", "SCR4b"))
        P.do("vector", lambda v, s4=s4: v.tensor_reduce(
            C.srow[:, 4 + s4:5 + s4], C.WF[:, 0:1024], AX.X, ALU.add,
            apply_absolute_value=True), reads=("SCR4a", "SCR4b"), writes=(f"sp{s4}",))
    for s4 in range(1, ns):
        P.do("vector", lambda v, s4=s4: v.tensor_tensor(
            C.srow[:, 4:5], C.srow[:, 4:5], C.srow[:, 4 + s4:5 + s4], ALU.add),
            reads=("sp0", f"sp{s4}"), writes=("sp0",))
    P.do("vector", lambda v: v.tensor_single_scalar(
        C.srow[:, 8:9], C.srow[:, 4:5], inv_n, ALU.mult), reads=("sp0",), writes=("sq8",))
    P.do("vector", lambda v: v.tensor_single_scalar(
        C.srow[:, 8:9], C.srow[:, 8:9], EPS, ALU.max), reads=("sq8",), writes=("sq8",))
    P.do("vector", lambda v: v.tensor_single_scalar(
        C.srow[:, 9:10], C.srow[:, 8:9], 0.5, ALU.mult), reads=("sq8",), writes=("sq9",))
    P.do("vector", lambda v: v.tensor_single_scalar(
        C.srow[:, 10:11], C.srow[:, 8:9], -0.5, ALU.mult), reads=("sq8",), writes=("sq10",))
    for s4 in range(ns):
        P.do("sync", lambda s, s4=s4: s.dma_start(
            out=C.WF[:, 0:1024], in_=wdram[row0:row0 + 128, 1024 * s4:1024 * (s4 + 1)]),
            reads=(), writes=("SCR4a", "SCR4b"))
        P.do("scalar", lambda s: s.activation(
            C.POOL[0][:, 0:1024], C.WF[:, 0:1024], AF.Sign, bias=C.srow[:, 10:11]),
            reads=("SCR4a", "SCR4b", "sq10"), writes=("PL0",))
        P.do("scalar", lambda s: s.activation(
            C.POOL[0][:, 1024:2048], C.WF[:, 0:1024], AF.Sign, bias=C.srow[:, 9:10]),
            reads=("SCR4a", "SCR4b", "sq9"), writes=("PL0",))
        P.do("gpsimd", lambda g: g.tensor_tensor(
            C.POOL[5][:, 0:1024], C.POOL[0][:, 0:1024], C.POOL[0][:, 1024:2048],
            ALU.add), reads=("PL0",), writes=("PL5a",))
        P.do("vector", lambda v: v.tensor_single_scalar(
            C.POOL[5][:, 1024:2048], C.POOL[5][:, 0:1024], C.srow[:, 9:10], ALU.mult),
            reads=("PL5a", "sq9"), writes=("PL5b",))
        out_store(s4, C.POOL[5][:, 1024:2048], "PL5b")


def stage_l0_ffn(core, W):
    P, C = core.P, core
    ln(core)
    for i in range(32):
        quant_rowtile(core, W["L0_ffn_w_fc"], 128 * i, 1024, 1.0 / 1024, 1,
                      lambda s4, ap, key, i=i: P.do("sync", lambda s, ap=ap, i=i: s.dma_start(
                          out=W["scrW"][128 * i:128 * (i + 1), :], in_=ap),
                          reads=(key,), writes=("SCRW",)))
    for i in range(8):
        quant_rowtile(core, W["L0_ffn_w_proj"], 128 * i, 4096, 1.0 / 4096, 4,
                      lambda s4, ap, key, i=i: P.do("sync", lambda s, ap=ap, i=i, s4=s4: s.dma_start(
                          out=W["scrW2"][128 * i:128 * (i + 1), 1024 * s4:1024 * (s4 + 1)],
                          in_=ap), reads=(key,), writes=("SCRW2",)))
    P.do("sync", lambda s: s.dma_start(out=C.WF[0:16, 0:1024], in_=W["L0_ffn_v"]),
         reads=(), writes=("SCR4a", "SCR4b"))
    P.do("vector", lambda v: v.tensor_copy(C.POOL[5][0:16, 0:1024], C.WF[0:16, 0:1024]),
         reads=("SCR4a", "SCR4b"), writes=("PL5a",))
    P.do("sync", lambda s: s.dma_start(out=W["scrSM"][0:16, 0:1024], in_=C.POOL[5][0:16, 0:1024]),
         reads=("PL5a",), writes=("SCRSM",))
    for i in range(8):
        P.do("sync", lambda s, i=i: s.dma_start(
            out=C.vT[i][:], in_=W["scrSM"][0:16, 128 * i:128 * (i + 1)], transpose=True),
            reads=("SCRSM",), writes=(f"vT{i}",))
    P.do("sync", lambda s: s.dma_start(out=C.WF[0:16, 0:16], in_=W["L0_ffn_fast_b"]),
         reads=(), writes=("SCR4a", "SCR4b"))
    P.do("vector", lambda v: v.tensor_copy(C.POOL[5][0:16, 0:16], C.WF[0:16, 0:16]),
         reads=("SCR4a", "SCR4b"), writes=("PL5a",))
    P.do("sync", lambda s: s.dma_start(out=W["scrSM"][16:32, 0:16], in_=C.POOL[5][0:16, 0:16]),
         reads=("PL5a",), writes=("SCRSM",))
    P.do("sync", lambda s: s.dma_start(out=C.fbT[:], in_=W["scrSM"][16:32, 0:128],
                                       transpose=True), reads=("SCRSM",), writes=("fbT",))
    for i in range(32):
        P.do("sync", lambda s, i=i: s.dma_start(
            out=C.WF[:, 0:16], in_=W["L0_ffn_u"][128 * i:128 * (i + 1), :]),
            reads=(), writes=("SCR4a", "SCR4b"))
        P.do("vector", lambda v, i=i: v.tensor_copy(
            C.POOL[5][:, 16 * i:16 * (i + 1)], C.WF[:, 0:16]),
            reads=("SCR4a", "SCR4b"), writes=("PL5a",))
    for i in range(32):
        P.do("sync", lambda s, i=i: s.dma_start(
            out=W["scrU"][128 * i:128 * (i + 1), 0:16], in_=C.POOL[5][:, 16 * i:16 * (i + 1)]),
            reads=("PL5a",), writes=("SCRU",))
    for half in range(2):
        P.do("sync", lambda s, half=half: s.dma_start(
            out=C.POOL[5][:, 0:2048],
            in_=W["scrU"][2048 * half:2048 * (half + 1), 0:128], transpose=True),
            reads=("SCRU",), writes=("PL5a", "PL5b"))
        P.do("vector", lambda v, half=half: v.tensor_copy(
            C.POOL[0][32 * half:32 * half + 16, 0:2048], C.POOL[5][0:16, 0:2048]),
            reads=("PL5a", "PL5b"), writes=("PL0",))
    P.do("sync", lambda s: s.dma_start(out=C.srow[0:1, 13:14], in_=W["L0_ffn_gate"]),
         reads=(), writes=("sq13",))
    P.do("tensor", lambda t: t.matmul(C.PC[:, 0:1], C.onesr[0:1, :], C.srow[0:1, 13:14],
                                      start=True, stop=True),
         reads=("sq13", "onesr"), writes=(pc(0),))
    P.do("vector", lambda v: v.tensor_copy(C.srow[:, 12:13], C.PC[:, 0:1]),
         reads=(pc(0),), writes=("sq12",))
    for half in range(2):
        P.do("vector", lambda v, half=half: v.tensor_single_scalar(
            C.POOL[0][32 * half:32 * half + 16, 0:2048],
            C.POOL[0][32 * half:32 * half + 16, 0:2048],
            C.srow[32 * half:32 * half + 16, 12:13], ALU.mult),
            reads=("PL0", "sq12"), writes=("PL0",))
    for tt in range(4):
        ts = slice(512 * tt, 512 * (tt + 1))
        for i in range(8):
            P.do("tensor", lambda t, i=i, ts=ts: t.matmul(
                C.PC[0:16, 0:512], C.vT[i][:], C.A[i][:, ts],
                start=(i == 0), stop=(i == 7)), reads=(f"vT{i}", f"A{i}"), writes=(pc(0),))
        P.do("vector", lambda v: v.tensor_copy(C.POOL[5][0:16, 0:512], C.PC[0:16, 0:512]),
             reads=(pc(0),), writes=("PL5a",))
        P.do("tensor", lambda t: t.matmul(
            C.PC[0:16, 512:1024], C.fbT[0:16, :], C.POOL[5][0:16, 0:512],
            start=True, stop=True), reads=("fbT", "PL5a"), writes=(pc(1),))
        P.do("vector", lambda v: v.tensor_copy(C.POOL[5][0:16, 512:1024], C.PC[0:16, 512:1024]),
             reads=(pc(1),), writes=("PL5a",))
        P.do("vector", lambda v: v.tensor_copy(C.POOL[5][32:48, 512:1024],
                                               C.POOL[5][0:16, 512:1024]),
             reads=("PL5a",), writes=("PL5a",))
        for g in range(4):
            for i in range(8):
                P.do("sync", lambda s, g=g, i=i: s.dma_start(
                    out=C.VT[i][:, 0:1024], in_=W["scrW"][1024 * g:1024 * (g + 1),
                                                          128 * i:128 * (i + 1)],
                    transpose=True), reads=("SCRW",), writes=(f"VT{i}",))
            for j in range(8):
                jj = 8 * g + j
                bank = jj % 4
                for i in range(8):
                    P.do("tensor", lambda t, i=i, j=j, bank=bank, ts=ts: t.matmul(
                        C.PB[:, 512 * bank:512 * (bank + 1)],
                        C.VT[i][:, 128 * j:128 * (j + 1)], C.A[i][:, ts],
                        start=(i == 0), stop=False), reads=(f"VT{i}", f"A{i}"),
                        writes=(pb(bank),))
                ur = 0 if jj < 16 else 32
                uc = (128 * jj) % 2048
                P.do("tensor", lambda t, ur=ur, uc=uc, bank=bank: t.matmul(
                    C.PB[:, 512 * bank:512 * (bank + 1)],
                    C.POOL[0][ur:ur + 16, uc:uc + 128],
                    C.POOL[5][ur:ur + 16, 512:1024],
                    start=False, stop=True), reads=("PL0", "PL5a"), writes=(pb(bank),))
                P.do("scalar", lambda s, bank=bank: s.activation(
                    C.FS[0], C.PB[:, 512 * bank:512 * (bank + 1)], AF.Relu),
                    reads=(pb(bank),), writes=("SCR4a", "SCR4b"))
                P.do("scalar", lambda s, jj=jj: s.activation(
                    C.B[jj // 4][:, 512 * (jj % 4):512 * (jj % 4 + 1)], C.FS[0],
                    AF.Square), reads=("SCR4a", "SCR4b"), writes=(f"B{jj // 4}",))
        for jj in range(32):
            ph = C.POOL[5][:, 1024 * (jj % 2):1024 * (jj % 2 + 1)]
            pkey = "PL5a" if jj % 2 == 0 else "PL5b"
            P.do("sync", lambda s, jj=jj, ph=ph: s.dma_start(
                out=ph, in_=W["scrW2"][:, 128 * jj:128 * (jj + 1)], transpose=True),
                reads=("SCRW2",), writes=(pkey,))
            for o in range(8):
                pbank = C.PB[:, 512 * (o % 4):512 * (o % 4 + 1)] if o < 4 else \
                    C.PC[:, 512 * (o % 4):512 * (o % 4 + 1)]
                bkey = pb(o % 4) if o < 4 else pc(o % 4)
                P.do("tensor", lambda t, jj=jj, o=o, ph=ph, pbank=pbank: t.matmul(
                    pbank, ph[:, 128 * o:128 * (o + 1)],
                    C.B[jj // 4][:, 512 * (jj % 4):512 * (jj % 4 + 1)],
                    start=(jj == 0), stop=(jj == 31)),
                    reads=(pkey, f"B{jj // 4}"), writes=(bkey,))
        for o in range(8):
            pbank = C.PB[:, 512 * (o % 4):512 * (o % 4 + 1)] if o < 4 else \
                C.PC[:, 512 * (o % 4):512 * (o % 4 + 1)]
            bkey = pb(o % 4) if o < 4 else pc(o % 4)
            P.do("vector", lambda v, o=o, ts=ts, pbank=pbank: v.tensor_tensor(
                C.X[o][:, ts], pbank, C.X[o][:, ts], ALU.add),
                reads=(bkey, f"X{o}"), writes=(f"X{o}",))


def quant_global(core, wdram, nrows, width, scr, scr_key):
    P, C = core.P, core
    nt = nrows // 128
    ns = width // 1024
    for i in range(nt):
        for s4 in range(ns):
            P.do("sync", lambda s, i=i, s4=s4: s.dma_start(
                out=C.WF[:, 0:1024],
                in_=wdram[128 * i:128 * (i + 1), 1024 * s4:1024 * (s4 + 1)]),
                reads=(), writes=("SCR4a", "SCR4b"))
            P.do("vector", lambda v: v.tensor_reduce(
                C.srow[:, 5:6], C.WF[:, 0:1024], AX.X, ALU.add,
                apply_absolute_value=True), reads=("SCR4a", "SCR4b"), writes=("sq5",))
            if i == 0 and s4 == 0:
                P.do("vector", lambda v: v.tensor_copy(C.srow[:, 4:5], C.srow[:, 5:6]),
                     reads=("sq5",), writes=("sp0",))
            else:
                P.do("vector", lambda v: v.tensor_tensor(
                    C.srow[:, 4:5], C.srow[:, 4:5], C.srow[:, 5:6], ALU.add),
                    reads=("sp0", "sq5"), writes=("sp0",))
    P.do("tensor", lambda t: t.matmul(C.PB[0:1, 0:1], C.ones[:, 0:1], C.srow[:, 4:5],
                                      start=True, stop=True),
         reads=("sp0", "ones"), writes=(pb(0),))
    P.do("vector", lambda v: v.tensor_copy(C.srow[0:1, 8:9], C.PB[0:1, 0:1]),
         reads=(pb(0),), writes=("sq8",))
    P.do("tensor", lambda t: t.matmul(C.PC[:, 0:1], C.onesr[0:1, :], C.srow[0:1, 8:9],
                                      start=True, stop=True),
         reads=("sq8", "onesr"), writes=(pc(0),))
    P.do("vector", lambda v, n=nrows * width: v.tensor_single_scalar(
        C.srow[:, 8:9], C.PC[:, 0:1], 1.0 / n, ALU.mult), reads=(pc(0),), writes=("sq8",))
    P.do("vector", lambda v: v.tensor_single_scalar(
        C.srow[:, 8:9], C.srow[:, 8:9], EPS, ALU.max), reads=("sq8",), writes=("sq8",))
    P.do("vector", lambda v: v.tensor_single_scalar(
        C.srow[:, 9:10], C.srow[:, 8:9], 0.5, ALU.mult), reads=("sq8",), writes=("sq9",))
    P.do("vector", lambda v: v.tensor_single_scalar(
        C.srow[:, 10:11], C.srow[:, 8:9], -0.5, ALU.mult), reads=("sq8",), writes=("sq10",))
    for i in range(nt):
        for s4 in range(ns):
            P.do("sync", lambda s, i=i, s4=s4: s.dma_start(
                out=C.WF[:, 0:1024],
                in_=wdram[128 * i:128 * (i + 1), 1024 * s4:1024 * (s4 + 1)]),
                reads=(), writes=("SCR4a", "SCR4b"))
            P.do("scalar", lambda s: s.activation(
                C.POOL[0][:, 0:1024], C.WF[:, 0:1024], AF.Sign, bias=C.srow[:, 10:11]),
                reads=("SCR4a", "SCR4b", "sq10"), writes=("PL0",))
            P.do("scalar", lambda s: s.activation(
                C.POOL[0][:, 1024:2048], C.WF[:, 0:1024], AF.Sign, bias=C.srow[:, 9:10]),
                reads=("SCR4a", "SCR4b", "sq9"), writes=("PL0",))
            P.do("gpsimd", lambda g: g.tensor_tensor(
                C.POOL[5][:, 0:1024], C.POOL[0][:, 0:1024], C.POOL[0][:, 1024:2048],
                ALU.add), reads=("PL0",), writes=("PL5a",))
            P.do("vector", lambda v: v.tensor_single_scalar(
                C.POOL[5][:, 1024:2048], C.POOL[5][:, 0:1024], C.srow[:, 9:10], ALU.mult),
                reads=("PL5a", "sq9"), writes=("PL5b",))
            P.do("sync", lambda s, i=i, s4=s4: s.dma_start(
                out=scr[128 * i:128 * (i + 1), 1024 * s4:1024 * (s4 + 1)],
                in_=C.POOL[5][:, 1024:2048]), reads=("PL5b",), writes=(scr_key,))


def act_quant_tile(core, src_ap, src_keys, dst_ap, dst_key, w):
    P, C = core.P, core
    P.do("vector", lambda v: v.tensor_tensor(C.FS[0], src_ap, C.bc[0][:], ALU.mult),
         reads=src_keys + ("bc0",), writes=("SCR4a",))
    P.do("vector", lambda v: v.tensor_single_scalar(C.IS[:], C.FS[0], 0.0, ALU.add),
         reads=("SCR4a",), writes=("IS",))
    P.do("gpsimd", lambda g: g.tensor_tensor(C.FS[1], C.FS[0], C.IS[:], ALU.subtract),
         reads=("SCR4a", "IS"), writes=("SCR4b",))
    P.do("vector", lambda v: v.scalar_tensor_tensor(
        C.FS[0], C.FS[1], 0.5, C.IS[:], ALU.is_gt, ALU.add),
        reads=("SCR4b", "IS"), writes=("SCR4a",))
    P.do("vector", lambda v: v.scalar_tensor_tensor(
        C.FS[1], C.FS[1], -0.5, C.FS[0], ALU.is_lt, ALU.subtract),
        reads=("SCR4b", "SCR4a"), writes=("SCR4b",))
    P.do("vector", lambda v: v.tensor_tensor(dst_ap, C.FS[1], C.bc[1][:], ALU.mult),
         reads=("SCR4b", "bc1"), writes=(dst_key,))


def amax_cols_to_scales(core, mx_ap, mx_keys, nslices, row_off):
    P, C = core.P, core
    for sl in range(nslices):
        P.do("tensor", lambda t, sl=sl: t.transpose(
            C.PB[:, 1536:1568], mx_ap[:, 128 * sl:128 * (sl + 1)], C.eye[0:32, 0:32]),
            reads=mx_keys + ("eye",), writes=(pb(3),))
        P.do("vector", lambda v: v.tensor_reduce(
            C.srow[:, 5:6], C.PB[:, 1536:1568], AX.X, ALU.max),
            reads=(pb(3),), writes=("sq5",))
        P.do("vector", lambda v: v.tensor_single_scalar(
            C.srow[:, 6:7], C.srow[:, 5:6], EPS, ALU.max), reads=("sq5",), writes=("sq6",))
        P.do("tensor", lambda t, sl=sl: t.transpose(
            C.PB[0:1, 1568:1696], C.srow[:, 6:7], C.eye[:, 0:128]),
            reads=("sq6", "eye"), writes=(pb(3),))
        P.do("vector", lambda v, sl=sl, row_off=row_off: v.tensor_copy(
            C.row[0][0:1, row_off + 128 * sl:row_off + 128 * (sl + 1)],
            C.PB[0:1, 1568:1696]), reads=(pb(3),), writes=("row0",))
    ro = slice(row_off, row_off + 128 * nslices)
    def _recip(v):
        with C.nc.allow_low_precision(reason="bf16 act-quant scale"):
            return v.reciprocal(C.row[1][0:1, ro], C.row[0][0:1, ro])
    P.do("vector", _recip, reads=("row0",), writes=("row1",))
    P.do("vector", lambda v: v.tensor_single_scalar(
        C.row[1][0:1, ro], C.row[1][0:1, ro], 127.0, ALU.mult),
        reads=("row1",), writes=("row1",))
    P.do("vector", lambda v: v.tensor_single_scalar(
        C.row[0][0:1, ro], C.row[0][0:1, ro], -1.0 / 127.0, ALU.mult),
        reads=("row0",), writes=("row0",))


def bcast_rows(core, w0, row_off):
    P, C = core.P, core
    P.do("tensor", lambda t: t.matmul(
        C.PB[:, 1024:1536], C.onesrb[0:1, :], C.row[1][0:1, row_off + w0:row_off + w0 + 512],
        start=True, stop=True), reads=("row1", "onesrb"), writes=(pb(2),))
    P.do("vector", lambda v: v.tensor_copy(C.bc[0][:], C.PB[:, 1024:1536]),
         reads=(pb(2),), writes=("bc0",))
    P.do("tensor", lambda t: t.matmul(
        C.PB[:, 1024:1536], C.onesrb[0:1, :], C.row[0][0:1, row_off + w0:row_off + w0 + 512],
        start=True, stop=True), reads=("row0", "onesrb"), writes=(pb(2),))
    P.do("vector", lambda v: v.tensor_copy(C.bc[1][:], C.PB[:, 1024:1536]),
         reads=(pb(2),), writes=("bc1",))


def stage_l1_ffn(core, W):
    P, C = core.P, core
    ln(core)
    quant_global(core, W["L1_ffn_w_gate"], FF, 1024, W["scrW"], "SCRW")
    quant_global(core, W["L1_ffn_w_up"], FF, 1024, W["scrW3"], "SCRW3")
    quant_global(core, W["L1_ffn_w_down"], 1024, FF, W["scrW2"], "SCRW2")
    P.do("vector", lambda v: v.tensor_copy(C.B[0][:], C.A[0][:]),
         reads=("A0",), writes=("B0",))
    P.do("gpsimd", lambda g: g.tensor_copy(C.B[1][:], C.A[0][:]),
         reads=("A0",), writes=("B1",))
    for c in range(1, 8):
        P.do("vector", lambda v, c=c: v.tensor_tensor(C.B[0][:], C.B[0][:], C.A[c][:],
                                                      ALU.max),
             reads=("B0", f"A{c}"), writes=("B0",))
        P.do("vector", lambda v, c=c: v.tensor_tensor(C.B[1][:], C.B[1][:], C.A[c][:],
                                                      ALU.min),
             reads=("B1", f"A{c}"), writes=("B1",))
    for (buf, bkey, op) in ((C.B[0], "B0", ALU.max), (C.B[1], "B1", ALU.min)):
        for (n0, n1) in ((64, 64), (32, 32)):
            P.do("vector", lambda v, buf=buf, n0=n0: v.tensor_copy(
                C.PB[0:n0, 0:S], buf[n0:2 * n0, :]),
                reads=(bkey,), writes=PBALL)
            P.do("vector", lambda v, buf=buf, op=op, n0=n0: v.tensor_tensor(
                buf[0:n0, :], buf[0:n0, :], C.PB[0:n0, 0:S], op),
                reads=(bkey,) + PBALL, writes=(bkey,))
    P.do("vector", lambda v: v.tensor_single_scalar(
        C.B[1][0:32, :], C.B[1][0:32, :], -1.0, ALU.mult), reads=("B1",), writes=("B1",))
    P.do("vector", lambda v: v.tensor_tensor(
        C.B[0][0:32, :], C.B[0][0:32, :], C.B[1][0:32, :], ALU.max),
        reads=("B0", "B1"), writes=("B0",))
    for half in range(2):
        P.do("vector", lambda v, half=half: v.tensor_copy(
            C.SCR4[0:32, 0:1024], C.B[0][0:32, 1024 * half:1024 * (half + 1)]),
            reads=("B0",), writes=("SCR4a", "SCR4b"))
        amax_cols_to_scales(core, C.SCR4[0:32, 0:1024], ("SCR4a", "SCR4b"), 8, 1024 * half)
    for w in range(4):
        bcast_rows(core, 512 * w, 0)
        for c in range(8):
            sl = slice(512 * w, 512 * (w + 1))
            act_quant_tile(core, C.A[c][:, sl], (f"A{c}",), C.A[c][:, sl], f"A{c}", w)
    for tt in range(4):
        ts = slice(512 * tt, 512 * (tt + 1))
        for g in range(4):
            for i in range(8):
                P.do("sync", lambda s, g=g, i=i: s.dma_start(
                    out=C.VT[i][:, 0:1024], in_=W["scrW"][1024 * g:1024 * (g + 1),
                                                          128 * i:128 * (i + 1)],
                    transpose=True), reads=("SCRW",), writes=(f"VT{i}",))
                P.do("sync", lambda s, g=g, i=i: s.dma_start(
                    out=C.VT[8 + i][:, 0:1024], in_=W["scrW3"][1024 * g:1024 * (g + 1),
                                                              128 * i:128 * (i + 1)],
                    transpose=True), reads=("SCRW3",), writes=(f"VT{8 + i}",))
            for j in range(8):
                jj = 8 * g + j
                bk_g = 2 * (jj % 2)
                bk_u = 2 * (jj % 2) + 1
                for i in range(8):
                    P.do("tensor", lambda t, i=i, j=j, bk_g=bk_g, ts=ts: t.matmul(
                        C.PB[:, 512 * bk_g:512 * (bk_g + 1)],
                        C.VT[i][:, 128 * j:128 * (j + 1)], C.A[i][:, ts],
                        start=(i == 0), stop=(i == 7)), reads=(f"VT{i}", f"A{i}"),
                        writes=(pb(bk_g),))
                    P.do("tensor", lambda t, i=i, j=j, bk_u=bk_u, ts=ts: t.matmul(
                        C.PB[:, 512 * bk_u:512 * (bk_u + 1)],
                        C.VT[8 + i][:, 128 * j:128 * (j + 1)], C.A[i][:, ts],
                        start=(i == 0), stop=(i == 7)), reads=(f"VT{8 + i}", f"A{i}"),
                        writes=(pb(bk_u),))
                P.do("scalar", lambda s, bk_g=bk_g: s.activation(
                    C.FS[0], C.PB[:, 512 * bk_g:512 * (bk_g + 1)], AF.Silu),
                    reads=(pb(bk_g),), writes=("SCR4a", "SCR4b"))
                P.do("vector", lambda v, jj=jj, bk_u=bk_u: v.tensor_tensor(
                    C.B[jj // 4][:, 512 * (jj % 4):512 * (jj % 4 + 1)], C.FS[0],
                    C.PB[:, 512 * bk_u:512 * (bk_u + 1)], ALU.mult),
                    reads=("SCR4a", pb(bk_u)), writes=(f"B{jj // 4}",))
        for jj in range(32):
            gt = C.B[jj // 4][:, 512 * (jj % 4):512 * (jj % 4 + 1)]
            if jj == 0:
                P.do("vector", lambda v, gt=gt: v.tensor_copy(C.FS[0], gt),
                     reads=(f"B{jj // 4}",), writes=("SCR4a",))
                P.do("gpsimd", lambda g, gt=gt: g.tensor_copy(C.FS[1], gt),
                     reads=(f"B{jj // 4}",), writes=("SCR4b",))
            else:
                P.do("vector", lambda v, gt=gt, jj=jj: v.tensor_tensor(
                    C.FS[0], C.FS[0], gt, ALU.max),
                    reads=("SCR4a", f"B{jj // 4}"), writes=("SCR4a",))
                P.do("vector", lambda v, gt=gt, jj=jj: v.tensor_tensor(
                    C.FS[1], C.FS[1], gt, ALU.min),
                    reads=("SCR4b", f"B{jj // 4}"), writes=("SCR4b",))
        for n0 in (64, 32):
            P.do("vector", lambda v, n0=n0: v.tensor_copy(
                C.PB[0:n0, 1536:2048], C.SCR4[n0:2 * n0, 0:512]),
                reads=("SCR4a",), writes=(pb(3),))
            P.do("vector", lambda v, n0=n0: v.tensor_tensor(
                C.SCR4[0:n0, 0:512], C.SCR4[0:n0, 0:512], C.PB[0:n0, 1536:2048],
                ALU.max), reads=("SCR4a",) + (pb(3),), writes=("SCR4a",))
        for n0 in (64, 32):
            P.do("vector", lambda v, n0=n0: v.tensor_copy(
                C.PB[0:n0, 1536:2048], C.FS[1][n0:2 * n0, :]),
                reads=("SCR4b",), writes=(pb(3),))
            P.do("vector", lambda v, n0=n0: v.tensor_tensor(
                C.FS[1][0:n0, :], C.FS[1][0:n0, :], C.PB[0:n0, 1536:2048],
                ALU.min), reads=("SCR4b",) + (pb(3),), writes=("SCR4b",))
        P.do("vector", lambda v: v.tensor_single_scalar(
            C.FS[1][0:32, :], C.FS[1][0:32, :], -1.0, ALU.mult),
            reads=("SCR4b",), writes=("SCR4b",))
        P.do("vector", lambda v: v.tensor_tensor(
            C.SCR4[0:32, 0:512], C.SCR4[0:32, 0:512], C.FS[1][0:32, :], ALU.max),
            reads=("SCR4a", "SCR4b"), writes=("SCR4a",))
        amax_cols_to_scales(core, C.SCR4[0:32, 0:512], ("SCR4a",), 4, 0)
        bcast_rows(core, 0, 0)
        for jj in range(32):
            gt = C.B[jj // 4][:, 512 * (jj % 4):512 * (jj % 4 + 1)]
            act_quant_tile(core, gt, (f"B{jj // 4}",), gt, f"B{jj // 4}", 0)
        for jj in range(32):
            ph = C.POOL[5][:, 1024 * (jj % 2):1024 * (jj % 2 + 1)]
            pkey = "PL5a" if jj % 2 == 0 else "PL5b"
            P.do("sync", lambda s, jj=jj, ph=ph: s.dma_start(
                out=ph, in_=W["scrW2"][:, 128 * jj:128 * (jj + 1)], transpose=True),
                reads=("SCRW2",), writes=(pkey,))
            for o in range(8):
                pbank = C.PB[:, 512 * (o % 4):512 * (o % 4 + 1)] if o < 4 else \
                    C.PC[:, 512 * (o % 4):512 * (o % 4 + 1)]
                bkey = pb(o % 4) if o < 4 else pc(o % 4)
                P.do("tensor", lambda t, jj=jj, o=o, ph=ph, pbank=pbank: t.matmul(
                    pbank, ph[:, 128 * o:128 * (o + 1)],
                    C.B[jj // 4][:, 512 * (jj % 4):512 * (jj % 4 + 1)],
                    start=(jj == 0), stop=(jj == 31)),
                    reads=(pkey, f"B{jj // 4}"), writes=(bkey,))
        for o in range(8):
            pbank = C.PB[:, 512 * (o % 4):512 * (o % 4 + 1)] if o < 4 else \
                C.PC[:, 512 * (o % 4):512 * (o % 4 + 1)]
            bkey = pb(o % 4) if o < 4 else pc(o % 4)
            P.do("vector", lambda v, o=o, ts=ts, pbank=pbank: v.tensor_tensor(
                C.X[o][:, ts], pbank, C.X[o][:, ts], ALU.add),
                reads=(bkey, f"X{o}"), writes=(f"X{o}",))



def _make_consts():
    k = np.arange(128)[:, None]
    q = np.arange(384)[None, :]
    mask = np.where(np.abs(q - 128 - k) <= 128, 0.0, NEG).astype(np.float32)
    ones = np.ones((128, 1), np.float32)
    onesr = np.ones((1, 128), np.float32)
    return {"c_eye": np.eye(128, dtype=np.float32), "c_ones": ones,
            "c_onesb": ones.astype(BF), "c_onesr": onesr,
            "c_onesrb": onesr.astype(BF), "c_mask": mask.astype(BF)}


_CONST_SPECS = [("c_eye", (128, 128), F32), ("c_ones", (128, 1), F32),
                ("c_onesb", (128, 1), BF16), ("c_onesr", (1, 128), F32),
                ("c_onesrb", (1, 128), BF16), ("c_mask", (128, 384), BF16)]

_STAGE_WEIGHTS = {
    "l0a": ["L0_attn_Wq", "L0_attn_Wk", "L0_attn_Wv", "L0_attn_Wo",
            "L0_attn_bq", "L0_attn_bk", "L0_attn_bv", "L0_attn_bo"],
    "l0f": ["L0_ffn_fast_b", "L0_ffn_w_fc", "L0_ffn_w_proj", "L0_ffn_u",
            "L0_ffn_v", "L0_ffn_gate"],
    "l1a": ["L1_attn_q_w", "L1_attn_q_b", "L1_attn_k_w", "L1_attn_k_b",
            "L1_attn_v_w", "L1_attn_v_b", "L1_attn_l_w", "L1_attn_l_b"],
    "l1f": ["L1_ffn_w_gate", "L1_ffn_w_up", "L1_ffn_w_down"],
}

_W_SHAPES = {
    "L0_attn_Wq": (D, D), "L0_attn_Wk": (D, D), "L0_attn_Wv": (D, D),
    "L0_attn_Wo": (D, D),
    "L0_attn_bq": (D, 1), "L0_attn_bk": (D, 1), "L0_attn_bv": (D, 1),
    "L0_attn_bo": (D, 1),
    "L0_ffn_fast_b": (16, 16), "L0_ffn_w_fc": (FF, D), "L0_ffn_w_proj": (D, FF),
    "L0_ffn_u": (FF, 16), "L0_ffn_v": (16, D), "L0_ffn_gate": (1, 1),
    "L1_attn_q_w": (D, D), "L1_attn_k_w": (D, D), "L1_attn_v_w": (D, D),
    "L1_attn_l_w": (D, D),
    "L1_attn_q_b": (D, 1), "L1_attn_k_b": (D, 1), "L1_attn_v_b": (D, 1),
    "L1_attn_l_b": (D, 1),
    "L1_ffn_w_gate": (FF, D), "L1_ffn_w_up": (FF, D), "L1_ffn_w_down": (D, FF),
}

_STAGE_FN = {"l0a": stage_l0_attn, "l0f": stage_l0_ffn, "l1a": stage_l1_attn,
             "l1f": stage_l1_ffn}


def _host_weights(inputs):
    w = {}
    for k, v in inputs.items():
        if k == "L0_attn_x":
            continue
        a = np.asarray(v)
        if k in ("L1_attn_q_w", "L1_attn_k_w", "L1_attn_v_w"):
            a = a.reshape(D, D)
        if k in _W_SHAPES:
            a = a.reshape(_W_SHAPES[k])
        w[k] = np.ascontiguousarray(a, dtype=np.float32)
    return w


def _build_program(stages):
    nc = bass.Bass("TRN2", target_bir_lowering=False)
    dram = {}
    names_in = ["x_in"]
    dram["x_in"] = nc.dram_tensor("x_in", [D, S], F32, kind="ExternalInput").ap()
    for cn, sh, dt in _CONST_SPECS:
        dram[cn] = nc.dram_tensor(cn, list(sh), dt, kind="ExternalInput").ap()
        names_in.append(cn)
    for st in stages:
        for n in _STAGE_WEIGHTS[st]:
            dram[n] = nc.dram_tensor(n, list(_W_SHAPES[n]), F32, kind="ExternalInput").ap()
            names_in.append(n)
    if "l0a" in stages or "l1a" in stages:
        dram["scrV"] = nc.dram_tensor("scrV", [D, S], BF16, kind="Internal").ap()
    if "l0f" in stages or "l1f" in stages or "l1a" in stages:
        dram["scrW"] = nc.dram_tensor("scrW", [FF, D], BF16, kind="Internal").ap()
        dram["scrW2"] = nc.dram_tensor("scrW2", [D, FF], BF16, kind="Internal").ap()
        dram["scrSM"] = nc.dram_tensor("scrSM", [32, 1024], BF16, kind="Internal").ap()
        dram["scrU"] = nc.dram_tensor("scrU", [FF, 128], BF16, kind="Internal").ap()
        dram["scrW3"] = nc.dram_tensor("scrW3", [FF, D], BF16, kind="Internal").ap()
    x_out = nc.dram_tensor("x_out", [D, S], F32, kind="ExternalOutput").ap()

    with ExitStack() as ctx:
        core = Core(nc, ctx)
        C, P = core, core.P
        for cn, sh, dt in _CONST_SPECS:
            C.dma(getattr(C, {"c_eye": "eye", "c_ones": "ones", "c_onesb": "onesb",
                              "c_onesr": "onesr", "c_onesrb": "onesrb",
                              "c_mask": "mask"}[cn])[:], dram[cn], cn)
        for c in range(8):
            C.dma(C.X[c][:], dram["x_in"][128 * c:128 * (c + 1), :], f"X{c}")
        for st in stages:
            _STAGE_FN[st](core, dram)
        tap_names = []
        for (tn, tfn, tsh, tdt, tkeys) in _DEBUG_TAPS:
            tout = nc.dram_tensor("tap_" + tn, list(tsh), tdt, kind="ExternalOutput").ap()
            P.do("sync", lambda s, tout=tout, tfn=tfn: s.dma_start(out=tout, in_=tfn(core)),
                 reads=tuple(tkeys), writes=("TAP" + tn,))
            tap_names.append(tn)
        for c in range(8):
            P.do("sync", lambda s, c=c: s.dma_start(
                out=x_out[128 * c:128 * (c + 1), :], in_=C.X[c][:]),
                reads=(f"X{c}",), writes=(f"XO{c}",))
        P.barrier("sync", [f"XO{c}" for c in range(8)] + ["TAP" + t for t in tap_names])
        with nc.Block() as blk:
            P.emit(blk, C.sems)
    return nc, names_in


def _run(stages, xs, w, consts):
    nc, names_in = _build_program(stages)
    in_maps = []
    for b in range(8):
        m = {"x_in": xs[b]}
        for k in names_in:
            if k.startswith("c_"):
                m[k] = consts[k]
            elif k != "x_in":
                m[k] = w[k]
        in_maps.append(m)
    res = run_bass_kernel_spmd(nc, in_maps, core_ids=list(range(8)))
    global _TAP_RESULTS
    _TAP_RESULTS = [{k: np.asarray(v) for k, v in r.items() if k.startswith("tap_")}
                    for r in res.results]
    return [np.asarray(r["x_out"]) for r in res.results]


def kernel(**inputs):
    consts = _make_consts()
    w = _host_weights(inputs)
    xs = [np.ascontiguousarray(np.asarray(inputs["L0_attn_x"][b], np.float32).T)
          for b in range(8)]
    if _FUSED:
        launches = [list(_STAGES)]
    else:
        launches = [[st] for st in _STAGES]
    for lst in launches:
        xs = _run(lst, xs, w, consts)
    return np.stack([x.T for x in xs]).astype(np.float32)


# version 03
# speedup vs baseline: 1.0002x; 1.0002x over previous
import numpy as np
import ml_dtypes
import concourse.bass as bass
import concourse.mybir as mybir
from concourse.bass_utils import run_bass_kernel_spmd
from contextlib import ExitStack

F32 = mybir.dt.float32
BF16 = mybir.dt.bfloat16
I32 = mybir.dt.int32
AF = mybir.ActivationFunctionType
ALU = mybir.AluOpType
AX = mybir.AxisListType
BF = ml_dtypes.bfloat16

D = 1024
S = 2048
NH = 16
FF = 4096
NEG = -1.0e30
EPS = 1e-5

_STAGES = ("l0a", "l0f", "l1a", "l1f")
_FUSED = True
_DEBUG_TAPS = []
_TAP_RESULTS = None
_SNAP_X = False
_STAGE_LIMIT = None


class Prog:
    ENGINES = ("sync", "gpsimd", "scalar", "vector", "tensor")
    STEP = {"sync": 16, "gpsimd": 16, "scalar": 1, "vector": 1, "tensor": 1}

    def __init__(self):
        self.count = {e: 0 for e in self.ENGINES}
        self.items = {e: [] for e in self.ENGINES}
        self.last_write = {}
        self.readers = {}
        self.waited = {e: {} for e in self.ENGINES}
        self.prev_writes = {e: set() for e in self.ENGINES}

    def do(self, eng, fn, reads=(), writes=()):
        waits = []

        def need(pch, pcount):
            if pch == eng and eng not in ("sync", "gpsimd"):
                return
            if self.waited[eng].get(pch, 0) >= pcount:
                return
            self.waited[eng][pch] = pcount
            waits.append((pch, pcount))

        for b in reads:
            lw = self.last_write.get(b)
            if lw:
                need(*lw)
        for b in writes:
            lw = self.last_write.get(b)
            if lw:
                need(*lw)
            for r in self.readers.get(b, ()):
                need(*r)
        if eng in ("scalar", "vector", "gpsimd") and \
                (set(reads) | set(writes)) & self.prev_writes[eng]:
            waits.append((eng, self.count[eng]))
        self.count[eng] += 1
        ca = self.count[eng]
        for b in writes:
            self.last_write[b] = (eng, ca)
            self.readers[b] = []
        for b in reads:
            self.readers.setdefault(b, []).append((eng, ca))
        self.prev_writes[eng] = set(writes)
        self.items[eng].append((waits, fn))

    def barrier(self, eng, bufs):
        waits = []
        for b in bufs:
            lw = self.last_write.get(b)
            if lw and lw[0] != eng and self.waited[eng].get(lw[0], 0) < lw[1]:
                self.waited[eng][lw[0]] = lw[1]
                waits.append(lw)
        if waits:
            self.items[eng].append((waits, None))

    def emit(self, block, sems):
        def run(section, en):
            for waits, fn in self.items[en]:
                for (pch, pcount) in waits:
                    section.wait_ge(sems[pch], pcount * self.STEP[pch])
                if fn is not None:
                    fn(section).then_inc(sems[en], self.STEP[en])
            if en == "sync":
                for pe in self.ENGINES:
                    section.wait_ge(sems[pe], self.count[pe] * self.STEP[pe])
                for pe in self.ENGINES:
                    section.sem_clear(sems[pe])
        block.sync(lambda s: run(s, "sync"))
        block.gpsimd(lambda g: run(g, "gpsimd"))
        block.scalar(lambda s: run(s, "scalar"))
        block.vector(lambda v: run(v, "vector"))
        block.tensor(lambda t: run(t, "tensor"))


def pb(i):
    return f"PB{i}"


def pc(i):
    return f"PC{i}"


PBALL = tuple(pb(i) for i in range(4))
PCALL = tuple(pc(i) for i in range(4))


class Core:
    def __init__(self, nc, ctx):
        E = ctx.enter_context
        self.nc = nc
        self.P = Prog()
        self.X = [E(nc.sbuf_tensor(f"x{c}", [128, S], F32)) for c in range(8)]
        self.A = [E(nc.sbuf_tensor(f"A{c}", [128, S], BF16)) for c in range(8)]
        self.B = [E(nc.sbuf_tensor(f"B{c}", [128, S], BF16)) for c in range(8)]
        self.VT = [E(nc.sbuf_tensor(f"VT{c}", [128, 1040], BF16, is_transpose_dest=True))
                   for c in range(16)]
        self.POOL = [E(nc.sbuf_tensor(f"PL{c}", [128, S], BF16, is_transpose_dest=True))
                     for c in range(6)]
        self.SCR4 = E(nc.sbuf_tensor("SCR4", [128, 1024], F32))
        self.WF = self.SCR4
        self.FS = [self.SCR4[:, 0:512], self.SCR4[:, 512:1024]]
        self.IS = E(nc.sbuf_tensor("IS0", [128, 512], I32))
        self.bias = E(nc.sbuf_tensor("biasc", [128, 72], F32))
        self.row = [E(nc.sbuf_tensor(f"row{i}", [1, S], BF16)) for i in range(2)]
        self.crow = E(nc.sbuf_tensor("crow", [1, 64], F32))
        self.bc = [E(nc.sbuf_tensor(f"bc{i}", [128, 512], BF16)) for i in range(2)]
        self.ones = E(nc.sbuf_tensor("ones", [128, 1], F32))
        self.onesb = E(nc.sbuf_tensor("onesb", [128, 1], BF16))
        self.onesr = E(nc.sbuf_tensor("onesr", [1, 128], F32))
        self.onesrb = E(nc.sbuf_tensor("onesrb", [1, 128], BF16))
        self.mask = E(nc.sbuf_tensor("maskT", [128, 384], BF16))
        self.eye = E(nc.sbuf_tensor("eye", [128, 128], F32))
        self.sc384 = E(nc.sbuf_tensor("sc384", [128, 384], F32))
        self.ex = [E(nc.sbuf_tensor(f"ex{i}", [128, 384], BF16)) for i in range(3)]
        self.vsc = E(nc.sbuf_tensor("vsc", [128, 64], BF16))
        self.srow = E(nc.sbuf_tensor("srow", [128, 16], F32))
        self.vT = [E(nc.sbuf_tensor(f"vT{i}", [128, 16], BF16, is_transpose_dest=True))
                   for i in range(8)]
        self.fbT = E(nc.sbuf_tensor("fbT", [128, 16], BF16, is_transpose_dest=True))
        self.PB = E(nc.psum_tensor([128, S], F32))
        self.PC = E(nc.psum_tensor([128, S], F32))
        self.sems = {e: E(nc.semaphore(f"sem_{e}")) for e in Prog.ENGINES}

    def dma(self, dst, src, key, reads=(), q="sync"):
        self.P.do(q, lambda s: s.dma_start(out=dst, in_=src), reads=reads, writes=(key,))


def pool_slot(core, i):
    return core.POOL[1 + i // 2][:, 1024 * (i % 2):1024 * (i % 2 + 1)], f"PL{1 + i // 2}"


def load_nat_w(core, wdram, nslots, slot_fn):
    P, C = core.P, core
    for i in range(nslots):
        dst, dkey = slot_fn(i)
        P.do("sync", lambda s, i=i: s.dma_start(
            out=C.WF[:, 0:1024], in_=wdram[128 * i:128 * (i + 1), :]),
            reads=(), writes=("SCR4a", "SCR4b"))
        P.do("vector", lambda v, dst=dst: v.tensor_copy(dst, C.WF[:, 0:1024]),
             reads=("SCR4a", "SCR4b"), writes=(dkey,))


def load_bias_col(core, bdram, slot, n=8):
    P, C = core.P, core
    for i in range(n):
        P.do("sync", lambda s, i=i: s.dma_start(
            out=C.bias[:, slot + i:slot + i + 1], in_=bdram[128 * i:128 * (i + 1), :]),
            reads=(), writes=(f"bias{slot + i}",))


def proj_1024(core, wkey_fmt, out_fn, bias_slot):
    P, C = core.P, core
    for co in range(8):
        for w in range(4):
            for i in range(8):
                sl, skey = pool_slot(core, i)
                P.do("tensor", lambda t, co=co, w=w, i=i, sl=sl: t.matmul(
                    C.PB[:, 512 * w:512 * (w + 1)], sl[:, 128 * co:128 * (co + 1)],
                    C.A[i][:, 512 * w:512 * (w + 1)], start=(i == 0), stop=(i == 7)),
                    reads=(skey, f"A{i}"), writes=(pb(w),))
            out_fn(co, w)


def ln(core):
    P, C = core.P, core
    sq = C.POOL[0]
    for c in range(8):
        P.do("gpsimd", lambda g, c=c: g.tensor_copy(C.A[c][:], C.X[c][:]),
             reads=(f"X{c}",), writes=(f"A{c}",))
    for c in range(8):
        P.do("scalar", lambda s, c=c: s.activation(sq[:], C.A[c][:], AF.Square),
             reads=(f"A{c}",), writes=("PL0",))
        for w in range(4):
            P.do("tensor", lambda t, c=c, w=w: t.matmul(
                C.PB[0:1, 512 * w:512 * (w + 1)], C.onesb[:, 0:1],
                C.A[c][:, 512 * w:512 * (w + 1)], start=(c == 0), stop=(c == 7)),
                reads=(f"A{c}", "onesb"), writes=(pb(w),))
            P.do("tensor", lambda t, c=c, w=w: t.matmul(
                C.PC[0:1, 512 * w:512 * (w + 1)], C.onesb[:, 0:1],
                sq[:, 512 * w:512 * (w + 1)], start=(c == 0), stop=(c == 7)),
                reads=("PL0", "onesb"), writes=(pc(w),))
    P.do("vector", lambda v: v.tensor_single_scalar(C.row[0][:], C.PB[0:1, :], 1.0 / D, ALU.mult),
         reads=PBALL, writes=("row0",))
    P.do("vector", lambda v: v.tensor_single_scalar(C.row[1][:], C.PC[0:1, :], 1.0 / D, ALU.mult),
         reads=PCALL, writes=("row1",))
    P.do("scalar", lambda s: s.activation(C.PB[0:1, 0:S], C.row[0][:], AF.Square),
         reads=("row0",), writes=PBALL)
    P.do("vector", lambda v: v.tensor_tensor(C.row[1][:], C.row[1][:], C.PB[0:1, 0:S], ALU.subtract),
         reads=("row1",) + PBALL, writes=("row1",))
    P.do("vector", lambda v: v.tensor_single_scalar(C.row[1][:], C.row[1][:], EPS, ALU.add),
         reads=("row1",), writes=("row1",))
    P.do("scalar", lambda s: s.activation(C.row[1][:], C.row[1][:], AF.Sqrt),
         reads=("row1",), writes=("row1",))
    def _recip_r1(v):
        with C.nc.allow_low_precision(reason="bf16 rstd row"):
            return v.reciprocal(C.row[1][:], C.row[1][:])
    P.do("vector", _recip_r1, reads=("row1",), writes=("row1",))
    for w in range(4):
        P.do("tensor", lambda t, w=w: t.matmul(
            C.PB[:, 0:512], C.onesrb[0:1, :], C.row[0][0:1, 512 * w:512 * (w + 1)],
            start=True, stop=True), reads=("row0", "onesrb"), writes=(pb(0),))
        P.do("vector", lambda v: v.tensor_copy(C.bc[0][:], C.PB[:, 0:512]),
             reads=(pb(0),), writes=("bc0",))
        P.do("tensor", lambda t, w=w: t.matmul(
            C.PB[:, 0:512], C.onesrb[0:1, :], C.row[1][0:1, 512 * w:512 * (w + 1)],
            start=True, stop=True), reads=("row1", "onesrb"), writes=(pb(0),))
        P.do("vector", lambda v: v.tensor_copy(C.bc[1][:], C.PB[:, 0:512]),
             reads=(pb(0),), writes=("bc1",))
        for c in range(8):
            P.do("vector", lambda v, c=c, w=w: v.tensor_tensor(
                C.FS[0], C.X[c][:, 512 * w:512 * (w + 1)], C.bc[0][:], ALU.subtract),
                reads=(f"X{c}", "bc0"), writes=("SCR4a",))
            P.do("gpsimd", lambda g, c=c, w=w: g.tensor_tensor(
                C.A[c][:, 512 * w:512 * (w + 1)], C.FS[0], C.bc[1][:], ALU.mult),
                reads=("SCR4a", "bc1"), writes=(f"A{c}",))


def head_norm_bound(core, sq_ap, sq_key, hh, crow_slot):
    P, C = core.P, core
    keys = sq_key if isinstance(sq_key, tuple) else (sq_key,)
    for w in range(4):
        P.do("tensor", lambda t, hh=hh, w=w: t.matmul(
            C.PC[0:1, 512 * w:512 * (w + 1)], C.onesb[64 * hh:64 * (hh + 1), 0:1],
            sq_ap[64 * hh:64 * (hh + 1), 512 * w:512 * (w + 1)],
            start=True, stop=True), reads=keys + ("onesb",), writes=(pc(w),))
    P.do("vector", lambda v, crow_slot=crow_slot: v.tensor_reduce(
        C.crow[0:1, crow_slot:crow_slot + 1], C.PC[0:1, :], AX.X, ALU.max),
        reads=PCALL, writes=(f"crow{crow_slot}",))


def stage_l0_attn(core, W):
    P, C = core.P, core
    ln(core)
    if _STAGE_LIMIT == "ln":
        return
    load_nat_w(core, W["L0_attn_Wv"], 8, lambda i: pool_slot(core, i))
    load_bias_col(core, W["L0_attn_bv"], 16)

    def v_out(co, w):
        P.do("vector", lambda v, co=co, w=w: v.tensor_single_scalar(
            C.POOL[5][:, 512 * w:512 * (w + 1)], C.PB[:, 512 * w:512 * (w + 1)],
            C.bias[:, 16 + co:17 + co], ALU.add),
            reads=(pb(w), f"bias{16 + co}"), writes=("PL5a" if w < 2 else "PL5b",))
        if w == 3:
            P.do("sync", lambda s, co=co: s.dma_start(
                out=W["scrV"][128 * co:128 * (co + 1), :], in_=C.POOL[5][:, :]),
                reads=("PL5a", "PL5b"), writes=("SCRV",))
    proj_1024(core, None, v_out, 16)
    for tt in range(16):
        P.do("sync", lambda s, tt=tt: s.dma_start(
            out=C.VT[tt][:, 0:1024], in_=W["scrV"][:, 128 * tt:128 * (tt + 1)],
            transpose=True), reads=("SCRV",), writes=(f"VT{tt}",))
    for tt in range(16):
        P.do("vector", lambda v, tt=tt: v.tensor_copy(C.POOL[5][:, 0:1024], C.VT[tt][:, 0:1024]),
             reads=(f"VT{tt}",), writes=("PL5a",))
        P.do("vector", lambda v, tt=tt: v.tensor_copy(
            C.VT[tt][:].rearrange("p (h e) -> p h e", h=16)[:, :, 0:64],
            C.POOL[5][:, 0:1024].rearrange("p (h e) -> p h e", h=16)),
            reads=("PL5a",), writes=(f"VT{tt}",))
        P.do("vector", lambda v, tt=tt: v.memset(
            C.VT[tt][:].rearrange("p (h e) -> p h e", h=16)[:, :, 64:65], 1.0),
            reads=(), writes=(f"VT{tt}",))
    if _STAGE_LIMIT == "v":
        return
    load_nat_w(core, W["L0_attn_Wk"], 8, lambda i: pool_slot(core, i))
    load_bias_col(core, W["L0_attn_bk"], 8)

    def k_out(co, w):
        P.do("vector", lambda v, co=co, w=w: v.tensor_single_scalar(
            C.B[co][:, 512 * w:512 * (w + 1)], C.PB[:, 512 * w:512 * (w + 1)],
            C.bias[:, 8 + co:9 + co], ALU.add),
            reads=(pb(w), f"bias{8 + co}"), writes=(f"B{co}",))
    proj_1024(core, None, k_out, 8)
    sq = C.POOL[0]
    for co in range(8):
        P.do("scalar", lambda s, co=co: s.activation(sq[:], C.B[co][:], AF.Square),
             reads=(f"B{co}",), writes=("PL0",))
        for hh in range(2):
            head_norm_bound(core, sq[:], "PL0", hh, 2 * co + hh)
    if _STAGE_LIMIT == "k":
        return
    load_nat_w(core, W["L0_attn_Wq"], 8, lambda i: pool_slot(core, i))
    load_bias_col(core, W["L0_attn_bq"], 0)
    qc = C.POOL[0]
    for co in range(8):
        for w in range(4):
            for i in range(8):
                sl, skey = pool_slot(core, i)
                P.do("tensor", lambda t, co=co, w=w, i=i, sl=sl: t.matmul(
                    C.PB[:, 512 * w:512 * (w + 1)], sl[:, 128 * co:128 * (co + 1)],
                    C.A[i][:, 512 * w:512 * (w + 1)], start=(i == 0), stop=(i == 7)),
                    reads=(skey, f"A{i}"), writes=(pb(w),))
            P.do("vector", lambda v, co=co, w=w: v.tensor_single_scalar(
                qc[:, 512 * w:512 * (w + 1)], C.PB[:, 512 * w:512 * (w + 1)],
                C.bias[:, co:co + 1], ALU.add),
                reads=(pb(w), f"bias{co}"), writes=("PL0",))
        P.do("scalar", lambda s: s.activation(C.POOL[5][:, 0:S], qc[:], AF.Square),
             reads=("PL0",), writes=("PL5a", "PL5b"))
        for hh in range(2):
            h = 2 * co + hh
            head_norm_bound(core, C.POOL[5][:, 0:S], ("PL5a", "PL5b"), hh, 32 + h)
            P.do("tensor", lambda t, h=h: t.matmul(
                C.PC[:, 0:1], C.onesr[0:1, :], C.crow[0:1, 32 + h:33 + h],
                start=True, stop=True), reads=(f"crow{32 + h}", "onesr"), writes=(pc(0),))
            P.do("tensor", lambda t, h=h: t.matmul(
                C.PC[:, 512:513], C.onesr[0:1, :], C.crow[0:1, h:h + 1],
                start=True, stop=True), reads=(f"crow{h}", "onesr"), writes=(pc(1),))
            P.do("vector", lambda v: v.tensor_copy(C.srow[:, 0:1], C.PC[:, 0:1]),
                 reads=(pc(0),), writes=("srow0",))
            P.do("vector", lambda v: v.tensor_tensor(
                C.srow[:, 1:2], C.srow[:, 0:1], C.PC[:, 512:513], ALU.mult),
                reads=("srow0", pc(1)), writes=("srow1",))
            P.do("scalar", lambda s: s.activation(
                C.srow[:, 2:3], C.srow[:, 1:2], AF.Sqrt, scale=1.0 / 64.0),
                reads=("srow1",), writes=("srow2",))
            P.do("vector", lambda v, h=h: v.tensor_single_scalar(
                C.bias[:, 32 + h:33 + h], C.srow[:, 2:3], -1.0, ALU.mult),
                reads=("srow2",), writes=(f"bias{32 + h}",))
        for hh in range(2):
            h = 2 * co + hh
            hb = 64 * hh
            for kb in range(16):
                j0 = 0 if kb > 0 else 128
                j1 = 384 if kb < 15 else 256
                q0 = 128 * (kb - 1)
                sb = kb % 2
                P.do("tensor", lambda t, co=co, hb=hb, kb=kb, j0=j0, j1=j1, q0=q0, sb=sb: t.matmul(
                    C.PB[:, 512 * sb + j0:512 * sb + j1],
                    C.B[co][hb:hb + 64, 128 * kb:128 * (kb + 1)],
                    qc[hb:hb + 64, q0 + j0:q0 + j1], start=True, stop=True),
                    reads=(f"B{co}", "PL0"), writes=(pb(sb),))
                scb = C.sc384[:, 0:384] if sb == 0 else C.SCR4[:, 0:384]
                sck = "sc384" if sb == 0 else "SCR4a"
                P.do("vector", lambda v, j0=j0, j1=j1, sb=sb, scb=scb: v.tensor_tensor(
                    scb[:, j0:j1], C.PB[:, 512 * sb + j0:512 * sb + j1],
                    C.mask[:, j0:j1], ALU.add),
                    reads=(pb(sb), "mask"), writes=(sck,))
                P.do("scalar", lambda s, j0=j0, j1=j1, h=h, kb=kb, scb=scb: s.activation(
                    C.ex[kb % 3][:, j0:j1], scb[:, j0:j1], AF.Exp,
                    bias=C.bias[:, 32 + h:33 + h], scale=0.125),
                    reads=(sck, f"bias{32 + h}"), writes=(f"ex{kb % 3}",))
                regs = [kb - 1] if kb > 0 else []
                if kb == 15:
                    regs.append(15)
                for r in regs:
                    ks = [k2 for k2 in (r - 1, r, r + 1) if 0 <= k2 <= 15]
                    for k2 in ks:
                        j = r - (k2 - 1)
                        P.do("tensor", lambda t, r=r, k2=k2, j=j, h=h, ks=tuple(ks): t.matmul(
                            C.PC[0:65, 128 * r:128 * (r + 1)],
                            C.VT[k2][:, 65 * h:65 * h + 65],
                            C.ex[k2 % 3][:, 128 * j:128 * (j + 1)],
                            start=(k2 == ks[0]), stop=(k2 == ks[-1])),
                            reads=(f"VT{k2}", f"ex{k2 % 3}"), writes=(pc(r // 4),))
            def _recip_s(v):
                with C.nc.allow_low_precision(reason="bf16 softmax-sum recip"):
                    return v.reciprocal(C.row[0][:], C.PC[64:65, :])
            P.do("vector", _recip_s, reads=PCALL, writes=("row0",))
            if _STAGE_LIMIT == "sums0" and co == 0 and hh == 0:
                return
            for w in range(4):
                P.do("tensor", lambda t, w=w: t.matmul(
                    C.PB[0:64, 1024:1536], C.onesrb[0:1, 0:64],
                    C.row[0][0:1, 512 * w:512 * (w + 1)], start=True, stop=True),
                    reads=("row0", "onesrb"), writes=(pb(2),))
                P.do("vector", lambda v: v.tensor_copy(C.bc[0][0:64, :], C.PB[0:64, 1024:1536]),
                     reads=(pb(2),), writes=("bc0",))
                P.do("vector", lambda v, w=w, co=co, hb=hb: v.tensor_tensor(
                    C.B[co][hb:hb + 64, 512 * w:512 * (w + 1)],
                    C.PC[0:64, 512 * w:512 * (w + 1)], C.bc[0][0:64, :], ALU.mult),
                    reads=(pc(w), "bc0"), writes=(f"B{co}",))
            if _STAGE_LIMIT == "h0" and co == 0 and hh == 0:
                return
        if _STAGE_LIMIT == "q0" and co == 0:
            return
    if _STAGE_LIMIT == "heads":
        return
    def vt_slot(i):
        return C.VT[i][:, 0:1024], f"VT{i}"
    load_nat_w(core, W["L0_attn_Wo"], 8, vt_slot)
    load_bias_col(core, W["L0_attn_bo"], 24)
    for o in range(8):
        for w in range(4):
            bank = w
            for i in range(8):
                P.do("tensor", lambda t, o=o, w=w, i=i, bank=bank: t.matmul(
                    C.PB[:, 512 * bank:512 * (bank + 1)], C.VT[i][:, 128 * o:128 * (o + 1)],
                    C.B[i][:, 512 * w:512 * (w + 1)], start=(i == 0), stop=(i == 7)),
                    reads=(f"VT{i}", f"B{i}"), writes=(pb(bank),))
            P.do("vector", lambda v, o=o, w=w, bank=bank: v.scalar_tensor_tensor(
                C.X[o][:, 512 * w:512 * (w + 1)], C.PB[:, 512 * bank:512 * (bank + 1)],
                C.bias[:, 24 + o:25 + o], C.X[o][:, 512 * w:512 * (w + 1)],
                ALU.add, ALU.add),
                reads=(pb(bank), f"bias{24 + o}", f"X{o}"), writes=(f"X{o}",))


def bounce_1024(core, wdram, scr, dst_slot_fn):
    P, C = core.P, core
    for i in range(8):
        P.do("sync", lambda s, i=i: s.dma_start(
            out=C.WF[:, 0:1024], in_=wdram[128 * i:128 * (i + 1), :]),
            reads=(), writes=("SCR4a", "SCR4b"))
        P.do("vector", lambda v: v.tensor_copy(C.POOL[5][:, 0:1024], C.WF[:, 0:1024]),
             reads=("SCR4a", "SCR4b"), writes=("PL5a",))
        P.do("sync", lambda s, i=i: s.dma_start(
            out=scr[128 * i:128 * (i + 1), 0:1024], in_=C.POOL[5][:, 0:1024]),
            reads=("PL5a",), writes=("SCRW",))
    for j in range(8):
        dst, dkey = dst_slot_fn(j)
        P.do("sync", lambda s, j=j, dst=dst: s.dma_start(
            out=dst, in_=scr[0:1024, 128 * j:128 * (j + 1)], transpose=True),
            reads=("SCRW",), writes=(dkey,))


def stage_l1_attn(core, W):
    P, C = core.P, core
    ln(core)
    if _STAGE_LIMIT == "ln":
        return
    bounce_1024(core, W["L1_attn_v_w"], W["scrW"], lambda j: pool_slot(core, j))
    load_bias_col(core, W["L1_attn_v_b"], 16)

    def v_out(co, w):
        P.do("vector", lambda v, co=co, w=w: v.tensor_single_scalar(
            C.POOL[5][:, 512 * w:512 * (w + 1)], C.PB[:, 512 * w:512 * (w + 1)],
            C.bias[:, 16 + co:17 + co], ALU.add),
            reads=(pb(w), f"bias{16 + co}"), writes=("PL5a" if w < 2 else "PL5b",))
        if w == 3:
            P.do("sync", lambda s, co=co: s.dma_start(
                out=W["scrV"][128 * co:128 * (co + 1), :], in_=C.POOL[5][:, :]),
                reads=("PL5a", "PL5b"), writes=("SCRV",))
    proj_1024(core, None, v_out, 16)
    for tt in range(16):
        P.do("sync", lambda s, tt=tt: s.dma_start(
            out=C.VT[tt][:, 0:1024], in_=W["scrV"][:, 128 * tt:128 * (tt + 1)],
            transpose=True), reads=("SCRV",), writes=(f"VT{tt}",))
    if _STAGE_LIMIT == "v":
        return
    bounce_1024(core, W["L1_attn_k_w"], W["scrW"], lambda j: pool_slot(core, j))
    load_bias_col(core, W["L1_attn_k_b"], 8)

    def k_out(co, w):
        P.do("vector", lambda v, co=co, w=w: v.tensor_single_scalar(
            C.B[co][:, 512 * w:512 * (w + 1)], C.PB[:, 512 * w:512 * (w + 1)],
            C.bias[:, 8 + co:9 + co], ALU.add),
            reads=(pb(w), f"bias{8 + co}"), writes=(f"B{co}",))
    proj_1024(core, None, k_out, 8)
    if _STAGE_LIMIT == "k":
        return
    bounce_1024(core, W["L1_attn_q_w"], W["scrW"], lambda j: pool_slot(core, j))
    load_bias_col(core, W["L1_attn_q_b"], 0)
    qc = C.POOL[0]
    for co in range(8):
        for w in range(4):
            for i in range(8):
                sl, skey = pool_slot(core, i)
                P.do("tensor", lambda t, co=co, w=w, i=i, sl=sl: t.matmul(
                    C.PB[:, 512 * w:512 * (w + 1)], sl[:, 128 * co:128 * (co + 1)],
                    C.A[i][:, 512 * w:512 * (w + 1)], start=(i == 0), stop=(i == 7)),
                    reads=(skey, f"A{i}"), writes=(pb(w),))
            P.do("vector", lambda v, co=co, w=w: v.tensor_single_scalar(
                qc[:, 512 * w:512 * (w + 1)], C.PB[:, 512 * w:512 * (w + 1)],
                C.bias[:, co:co + 1], ALU.add),
                reads=(pb(w), f"bias{co}"), writes=("PL0",))
        for hh in range(2):
            h = 2 * co + hh
            hb = 64 * hh
            for kc in range(16):
                for w in range(4):
                    P.do("tensor", lambda t, co=co, hb=hb, kc=kc, w=w: t.matmul(
                        C.PB[:, 512 * w:512 * (w + 1)],
                        C.B[co][hb:hb + 64, 128 * kc:128 * (kc + 1)],
                        qc[hb:hb + 64, 512 * w:512 * (w + 1)], start=True, stop=True),
                        reads=(f"B{co}", "PL0"), writes=(pb(w),))
                P.do("vector", lambda v: v.tensor_reduce(
                    C.srow[:, 5:6], C.PB[:, 0:S], AX.X, ALU.max, negate=True),
                    reads=PBALL, writes=("srow5",))
                P.do("vector", lambda v: v.tensor_single_scalar(
                    C.srow[:, 6:7], C.srow[:, 5:6], 0.125, ALU.mult),
                    reads=("srow5",), writes=("srow6",))
                P.do("scalar", lambda s: s.activation(
                    C.POOL[5][:, 0:S], C.PB[:, 0:S], AF.Exp,
                    bias=C.srow[:, 6:7], scale=0.125,
                    accum_out=C.srow[:, 0:1]),
                    reads=PBALL + ("srow6",), writes=("PL5a", "PL5b", "srow0"))
                P.do("vector", lambda v: v.reciprocal(C.srow[:, 1:2], C.srow[:, 0:1]),
                     reads=("srow0",), writes=("srow1",))
                P.do("vector", lambda v, kc=kc, h=h: v.tensor_single_scalar(
                    C.vsc[:], C.VT[kc][:, 64 * h:64 * h + 64],
                    C.srow[:, 1:2], ALU.mult),
                    reads=(f"VT{kc}", "srow1"), writes=("vsc",))
                for w in range(4):
                    P.do("tensor", lambda t, kc=kc, w=w: t.matmul(
                        C.PC[0:64, 512 * w:512 * (w + 1)], C.vsc[:],
                        C.POOL[5][:, 512 * w:512 * (w + 1)],
                        start=(kc == 0), stop=(kc == 15)),
                        reads=("vsc", "PL5a" if w < 2 else "PL5b"), writes=(pc(w),))
            for w in range(4):
                P.do("vector", lambda v, co=co, hb=hb, w=w: v.tensor_copy(
                    C.B[co][hb:hb + 64, 512 * w:512 * (w + 1)],
                    C.PC[0:64, 512 * w:512 * (w + 1)]),
                    reads=(pc(w),), writes=(f"B{co}",))
            if _STAGE_LIMIT == "h0" and co == 0 and hh == 0:
                return
        if _STAGE_LIMIT == "q0" and co == 0:
            return
    if _STAGE_LIMIT == "heads":
        return
    def vt_slot(i):
        return C.VT[i][:, 0:1024], f"VT{i}"
    bounce_1024(core, W["L1_attn_l_w"], W["scrW"], vt_slot)
    load_bias_col(core, W["L1_attn_l_b"], 24)
    for o in range(8):
        for w in range(4):
            bank = w
            for i in range(8):
                P.do("tensor", lambda t, o=o, w=w, i=i, bank=bank: t.matmul(
                    C.PB[:, 512 * bank:512 * (bank + 1)], C.VT[i][:, 128 * o:128 * (o + 1)],
                    C.B[i][:, 512 * w:512 * (w + 1)], start=(i == 0), stop=(i == 7)),
                    reads=(f"VT{i}", f"B{i}"), writes=(pb(bank),))
            P.do("vector", lambda v, o=o, w=w, bank=bank: v.scalar_tensor_tensor(
                C.X[o][:, 512 * w:512 * (w + 1)], C.PB[:, 512 * bank:512 * (bank + 1)],
                C.bias[:, 24 + o:25 + o], C.X[o][:, 512 * w:512 * (w + 1)],
                ALU.add, ALU.add),
                reads=(pb(bank), f"bias{24 + o}", f"X{o}"), writes=(f"X{o}",))


def quant_rowtile(core, wdram, row0, width, inv_n, stage_tiles, out_store):
    P, C = core.P, core
    ns = width // 1024
    for s4 in range(ns):
        P.do("sync", lambda s, s4=s4: s.dma_start(
            out=C.WF[:, 0:1024], in_=wdram[row0:row0 + 128, 1024 * s4:1024 * (s4 + 1)]),
            reads=(), writes=("SCR4a", "SCR4b"))
        P.do("vector", lambda v, s4=s4: v.tensor_reduce(
            C.srow[:, 4 + s4:5 + s4], C.WF[:, 0:1024], AX.X, ALU.add,
            apply_absolute_value=True), reads=("SCR4a", "SCR4b"), writes=(f"sp{s4}",))
    for s4 in range(1, ns):
        P.do("vector", lambda v, s4=s4: v.tensor_tensor(
            C.srow[:, 4:5], C.srow[:, 4:5], C.srow[:, 4 + s4:5 + s4], ALU.add),
            reads=("sp0", f"sp{s4}"), writes=("sp0",))
    P.do("vector", lambda v: v.tensor_single_scalar(
        C.srow[:, 8:9], C.srow[:, 4:5], inv_n, ALU.mult), reads=("sp0",), writes=("sq8",))
    P.do("vector", lambda v: v.tensor_single_scalar(
        C.srow[:, 8:9], C.srow[:, 8:9], EPS, ALU.max), reads=("sq8",), writes=("sq8",))
    P.do("vector", lambda v: v.tensor_single_scalar(
        C.srow[:, 9:10], C.srow[:, 8:9], 0.5, ALU.mult), reads=("sq8",), writes=("sq9",))
    P.do("vector", lambda v: v.tensor_single_scalar(
        C.srow[:, 10:11], C.srow[:, 8:9], -0.5, ALU.mult), reads=("sq8",), writes=("sq10",))
    for s4 in range(ns):
        P.do("sync", lambda s, s4=s4: s.dma_start(
            out=C.WF[:, 0:1024], in_=wdram[row0:row0 + 128, 1024 * s4:1024 * (s4 + 1)]),
            reads=(), writes=("SCR4a", "SCR4b"))
        P.do("scalar", lambda s: s.activation(
            C.POOL[0][:, 0:1024], C.WF[:, 0:1024], AF.Sign, bias=C.srow[:, 10:11]),
            reads=("SCR4a", "SCR4b", "sq10"), writes=("PL0",))
        P.do("scalar", lambda s: s.activation(
            C.POOL[0][:, 1024:2048], C.WF[:, 0:1024], AF.Sign, bias=C.srow[:, 9:10]),
            reads=("SCR4a", "SCR4b", "sq9"), writes=("PL0",))
        P.do("gpsimd", lambda g: g.tensor_tensor(
            C.POOL[5][:, 0:1024], C.POOL[0][:, 0:1024], C.POOL[0][:, 1024:2048],
            ALU.add), reads=("PL0",), writes=("PL5a",))
        P.do("vector", lambda v: v.tensor_single_scalar(
            C.POOL[5][:, 1024:2048], C.POOL[5][:, 0:1024], C.srow[:, 9:10], ALU.mult),
            reads=("PL5a", "sq9"), writes=("PL5b",))
        out_store(s4, C.POOL[5][:, 1024:2048], "PL5b")


def stage_l0_ffn(core, W):
    P, C = core.P, core
    ln(core)
    for i in range(32):
        quant_rowtile(core, W["L0_ffn_w_fc"], 128 * i, 1024, 1.0 / 1024, 1,
                      lambda s4, ap, key, i=i: P.do("sync", lambda s, ap=ap, i=i: s.dma_start(
                          out=W["scrW"][128 * i:128 * (i + 1), :], in_=ap),
                          reads=(key,), writes=("SCRW",)))
    for i in range(8):
        quant_rowtile(core, W["L0_ffn_w_proj"], 128 * i, 4096, 1.0 / 4096, 4,
                      lambda s4, ap, key, i=i: P.do("sync", lambda s, ap=ap, i=i, s4=s4: s.dma_start(
                          out=W["scrW2"][128 * i:128 * (i + 1), 1024 * s4:1024 * (s4 + 1)],
                          in_=ap), reads=(key,), writes=("SCRW2",)))
    P.do("sync", lambda s: s.dma_start(out=C.WF[0:16, 0:1024], in_=W["L0_ffn_v"]),
         reads=(), writes=("SCR4a", "SCR4b"))
    P.do("vector", lambda v: v.tensor_copy(C.POOL[5][0:16, 0:1024], C.WF[0:16, 0:1024]),
         reads=("SCR4a", "SCR4b"), writes=("PL5a",))
    P.do("sync", lambda s: s.dma_start(out=W["scrSM"][0:16, 0:1024], in_=C.POOL[5][0:16, 0:1024]),
         reads=("PL5a",), writes=("SCRSM",))
    for i in range(8):
        P.do("sync", lambda s, i=i: s.dma_start(
            out=C.vT[i][:], in_=W["scrSM"][0:16, 128 * i:128 * (i + 1)], transpose=True),
            reads=("SCRSM",), writes=(f"vT{i}",))
    P.do("sync", lambda s: s.dma_start(out=C.WF[0:16, 0:16], in_=W["L0_ffn_fast_b"]),
         reads=(), writes=("SCR4a", "SCR4b"))
    P.do("vector", lambda v: v.tensor_copy(C.POOL[5][0:16, 0:16], C.WF[0:16, 0:16]),
         reads=("SCR4a", "SCR4b"), writes=("PL5a",))
    P.do("sync", lambda s: s.dma_start(out=W["scrSM"][16:32, 0:16], in_=C.POOL[5][0:16, 0:16]),
         reads=("PL5a",), writes=("SCRSM",))
    P.do("sync", lambda s: s.dma_start(out=C.fbT[:], in_=W["scrSM"][16:32, 0:128],
                                       transpose=True), reads=("SCRSM",), writes=("fbT",))
    for i in range(32):
        P.do("sync", lambda s, i=i: s.dma_start(
            out=C.WF[:, 0:16], in_=W["L0_ffn_u"][128 * i:128 * (i + 1), :]),
            reads=(), writes=("SCR4a", "SCR4b"))
        P.do("vector", lambda v, i=i: v.tensor_copy(
            C.POOL[5][:, 16 * i:16 * (i + 1)], C.WF[:, 0:16]),
            reads=("SCR4a", "SCR4b"), writes=("PL5a",))
    for i in range(32):
        P.do("sync", lambda s, i=i: s.dma_start(
            out=W["scrU"][128 * i:128 * (i + 1), 0:16], in_=C.POOL[5][:, 16 * i:16 * (i + 1)]),
            reads=("PL5a",), writes=("SCRU",))
    for half in range(2):
        P.do("sync", lambda s, half=half: s.dma_start(
            out=C.POOL[5][:, 0:2048],
            in_=W["scrU"][2048 * half:2048 * (half + 1), 0:128], transpose=True),
            reads=("SCRU",), writes=("PL5a", "PL5b"))
        P.do("vector", lambda v, half=half: v.tensor_copy(
            C.POOL[0][32 * half:32 * half + 16, 0:2048], C.POOL[5][0:16, 0:2048]),
            reads=("PL5a", "PL5b"), writes=("PL0",))
    P.do("sync", lambda s: s.dma_start(out=C.srow[0:1, 13:14], in_=W["L0_ffn_gate"]),
         reads=(), writes=("sq13",))
    P.do("tensor", lambda t: t.matmul(C.PC[:, 0:1], C.onesr[0:1, :], C.srow[0:1, 13:14],
                                      start=True, stop=True),
         reads=("sq13", "onesr"), writes=(pc(0),))
    P.do("vector", lambda v: v.tensor_copy(C.srow[:, 12:13], C.PC[:, 0:1]),
         reads=(pc(0),), writes=("sq12",))
    for half in range(2):
        P.do("vector", lambda v, half=half: v.tensor_single_scalar(
            C.POOL[0][32 * half:32 * half + 16, 0:2048],
            C.POOL[0][32 * half:32 * half + 16, 0:2048],
            C.srow[32 * half:32 * half + 16, 12:13], ALU.mult),
            reads=("PL0", "sq12"), writes=("PL0",))
    for tt in range(4):
        ts = slice(512 * tt, 512 * (tt + 1))
        for i in range(8):
            P.do("tensor", lambda t, i=i, ts=ts: t.matmul(
                C.PC[0:16, 0:512], C.vT[i][:], C.A[i][:, ts],
                start=(i == 0), stop=(i == 7)), reads=(f"vT{i}", f"A{i}"), writes=(pc(0),))
        P.do("vector", lambda v: v.tensor_copy(C.POOL[5][0:16, 0:512], C.PC[0:16, 0:512]),
             reads=(pc(0),), writes=("PL5a",))
        P.do("tensor", lambda t: t.matmul(
            C.PC[0:16, 512:1024], C.fbT[0:16, :], C.POOL[5][0:16, 0:512],
            start=True, stop=True), reads=("fbT", "PL5a"), writes=(pc(1),))
        P.do("vector", lambda v: v.tensor_copy(C.POOL[5][0:16, 512:1024], C.PC[0:16, 512:1024]),
             reads=(pc(1),), writes=("PL5a",))
        P.do("vector", lambda v: v.tensor_copy(C.POOL[5][32:48, 512:1024],
                                               C.POOL[5][0:16, 512:1024]),
             reads=("PL5a",), writes=("PL5a",))
        for g in range(4):
            for i in range(8):
                P.do("sync", lambda s, g=g, i=i: s.dma_start(
                    out=C.VT[i][:, 0:1024], in_=W["scrW"][1024 * g:1024 * (g + 1),
                                                          128 * i:128 * (i + 1)],
                    transpose=True), reads=("SCRW",), writes=(f"VT{i}",))
            for j in range(8):
                jj = 8 * g + j
                bank = jj % 4
                for i in range(8):
                    P.do("tensor", lambda t, i=i, j=j, bank=bank, ts=ts: t.matmul(
                        C.PB[:, 512 * bank:512 * (bank + 1)],
                        C.VT[i][:, 128 * j:128 * (j + 1)], C.A[i][:, ts],
                        start=(i == 0), stop=False), reads=(f"VT{i}", f"A{i}"),
                        writes=(pb(bank),))
                ur = 0 if jj < 16 else 32
                uc = (128 * jj) % 2048
                P.do("tensor", lambda t, ur=ur, uc=uc, bank=bank: t.matmul(
                    C.PB[:, 512 * bank:512 * (bank + 1)],
                    C.POOL[0][ur:ur + 16, uc:uc + 128],
                    C.POOL[5][ur:ur + 16, 512:1024],
                    start=False, stop=True), reads=("PL0", "PL5a"), writes=(pb(bank),))
                P.do("scalar", lambda s, bank=bank: s.activation(
                    C.FS[0], C.PB[:, 512 * bank:512 * (bank + 1)], AF.Relu),
                    reads=(pb(bank),), writes=("SCR4a", "SCR4b"))
                P.do("scalar", lambda s, jj=jj: s.activation(
                    C.B[jj // 4][:, 512 * (jj % 4):512 * (jj % 4 + 1)], C.FS[0],
                    AF.Square), reads=("SCR4a", "SCR4b"), writes=(f"B{jj // 4}",))
        for jj in range(32):
            ph = C.POOL[5][:, 1024 * (jj % 2):1024 * (jj % 2 + 1)]
            pkey = "PL5a" if jj % 2 == 0 else "PL5b"
            P.do("sync", lambda s, jj=jj, ph=ph: s.dma_start(
                out=ph, in_=W["scrW2"][:, 128 * jj:128 * (jj + 1)], transpose=True),
                reads=("SCRW2",), writes=(pkey,))
            for o in range(8):
                pbank = C.PB[:, 512 * (o % 4):512 * (o % 4 + 1)] if o < 4 else \
                    C.PC[:, 512 * (o % 4):512 * (o % 4 + 1)]
                bkey = pb(o % 4) if o < 4 else pc(o % 4)
                P.do("tensor", lambda t, jj=jj, o=o, ph=ph, pbank=pbank: t.matmul(
                    pbank, ph[:, 128 * o:128 * (o + 1)],
                    C.B[jj // 4][:, 512 * (jj % 4):512 * (jj % 4 + 1)],
                    start=(jj == 0), stop=(jj == 31)),
                    reads=(pkey, f"B{jj // 4}"), writes=(bkey,))
        for o in range(8):
            pbank = C.PB[:, 512 * (o % 4):512 * (o % 4 + 1)] if o < 4 else \
                C.PC[:, 512 * (o % 4):512 * (o % 4 + 1)]
            bkey = pb(o % 4) if o < 4 else pc(o % 4)
            P.do("vector", lambda v, o=o, ts=ts, pbank=pbank: v.tensor_tensor(
                C.X[o][:, ts], pbank, C.X[o][:, ts], ALU.add),
                reads=(bkey, f"X{o}"), writes=(f"X{o}",))


def quant_global(core, wdram, nrows, width, scr, scr_key):
    P, C = core.P, core
    nt = nrows // 128
    ns = width // 1024
    for i in range(nt):
        for s4 in range(ns):
            P.do("sync", lambda s, i=i, s4=s4: s.dma_start(
                out=C.WF[:, 0:1024],
                in_=wdram[128 * i:128 * (i + 1), 1024 * s4:1024 * (s4 + 1)]),
                reads=(), writes=("SCR4a", "SCR4b"))
            P.do("vector", lambda v: v.tensor_reduce(
                C.srow[:, 5:6], C.WF[:, 0:1024], AX.X, ALU.add,
                apply_absolute_value=True), reads=("SCR4a", "SCR4b"), writes=("sq5",))
            if i == 0 and s4 == 0:
                P.do("vector", lambda v: v.tensor_copy(C.srow[:, 4:5], C.srow[:, 5:6]),
                     reads=("sq5",), writes=("sp0",))
            else:
                P.do("vector", lambda v: v.tensor_tensor(
                    C.srow[:, 4:5], C.srow[:, 4:5], C.srow[:, 5:6], ALU.add),
                    reads=("sp0", "sq5"), writes=("sp0",))
    P.do("tensor", lambda t: t.matmul(C.PB[0:1, 0:1], C.ones[:, 0:1], C.srow[:, 4:5],
                                      start=True, stop=True),
         reads=("sp0", "ones"), writes=(pb(0),))
    P.do("vector", lambda v: v.tensor_copy(C.srow[0:1, 8:9], C.PB[0:1, 0:1]),
         reads=(pb(0),), writes=("sq8",))
    P.do("tensor", lambda t: t.matmul(C.PC[:, 0:1], C.onesr[0:1, :], C.srow[0:1, 8:9],
                                      start=True, stop=True),
         reads=("sq8", "onesr"), writes=(pc(0),))
    P.do("vector", lambda v, n=nrows * width: v.tensor_single_scalar(
        C.srow[:, 8:9], C.PC[:, 0:1], 1.0 / n, ALU.mult), reads=(pc(0),), writes=("sq8",))
    P.do("vector", lambda v: v.tensor_single_scalar(
        C.srow[:, 8:9], C.srow[:, 8:9], EPS, ALU.max), reads=("sq8",), writes=("sq8",))
    P.do("vector", lambda v: v.tensor_single_scalar(
        C.srow[:, 9:10], C.srow[:, 8:9], 0.5, ALU.mult), reads=("sq8",), writes=("sq9",))
    P.do("vector", lambda v: v.tensor_single_scalar(
        C.srow[:, 10:11], C.srow[:, 8:9], -0.5, ALU.mult), reads=("sq8",), writes=("sq10",))
    for i in range(nt):
        for s4 in range(ns):
            P.do("sync", lambda s, i=i, s4=s4: s.dma_start(
                out=C.WF[:, 0:1024],
                in_=wdram[128 * i:128 * (i + 1), 1024 * s4:1024 * (s4 + 1)]),
                reads=(), writes=("SCR4a", "SCR4b"))
            P.do("scalar", lambda s: s.activation(
                C.POOL[0][:, 0:1024], C.WF[:, 0:1024], AF.Sign, bias=C.srow[:, 10:11]),
                reads=("SCR4a", "SCR4b", "sq10"), writes=("PL0",))
            P.do("scalar", lambda s: s.activation(
                C.POOL[0][:, 1024:2048], C.WF[:, 0:1024], AF.Sign, bias=C.srow[:, 9:10]),
                reads=("SCR4a", "SCR4b", "sq9"), writes=("PL0",))
            P.do("gpsimd", lambda g: g.tensor_tensor(
                C.POOL[5][:, 0:1024], C.POOL[0][:, 0:1024], C.POOL[0][:, 1024:2048],
                ALU.add), reads=("PL0",), writes=("PL5a",))
            P.do("vector", lambda v: v.tensor_single_scalar(
                C.POOL[5][:, 1024:2048], C.POOL[5][:, 0:1024], C.srow[:, 9:10], ALU.mult),
                reads=("PL5a", "sq9"), writes=("PL5b",))
            P.do("sync", lambda s, i=i, s4=s4: s.dma_start(
                out=scr[128 * i:128 * (i + 1), 1024 * s4:1024 * (s4 + 1)],
                in_=C.POOL[5][:, 1024:2048]), reads=("PL5b",), writes=(scr_key,))


def act_quant_tile(core, src_ap, src_keys, dst_ap, dst_key, w):
    P, C = core.P, core
    P.do("vector", lambda v: v.tensor_tensor(C.FS[0], src_ap, C.bc[0][:], ALU.mult),
         reads=src_keys + ("bc0",), writes=("SCR4a",))
    P.do("vector", lambda v: v.tensor_single_scalar(C.IS[:], C.FS[0], 0.0, ALU.add),
         reads=("SCR4a",), writes=("IS",))
    P.do("gpsimd", lambda g: g.tensor_tensor(C.FS[1], C.FS[0], C.IS[:], ALU.subtract),
         reads=("SCR4a", "IS"), writes=("SCR4b",))
    P.do("vector", lambda v: v.scalar_tensor_tensor(
        C.FS[0], C.FS[1], 0.5, C.IS[:], ALU.is_gt, ALU.add),
        reads=("SCR4b", "IS"), writes=("SCR4a",))
    P.do("vector", lambda v: v.scalar_tensor_tensor(
        C.FS[1], C.FS[1], -0.5, C.FS[0], ALU.is_lt, ALU.subtract),
        reads=("SCR4b", "SCR4a"), writes=("SCR4b",))
    P.do("vector", lambda v: v.tensor_tensor(dst_ap, C.FS[1], C.bc[1][:], ALU.mult),
         reads=("SCR4b", "bc1"), writes=(dst_key,))


def amax_cols_to_scales(core, mx_ap, mx_keys, nslices, row_off):
    P, C = core.P, core
    for sl in range(nslices):
        P.do("tensor", lambda t, sl=sl: t.transpose(
            C.PB[:, 1536:1568], mx_ap[:, 128 * sl:128 * (sl + 1)], C.eye[0:32, 0:32]),
            reads=mx_keys + ("eye",), writes=(pb(3),))
        P.do("vector", lambda v: v.tensor_reduce(
            C.srow[:, 5:6], C.PB[:, 1536:1568], AX.X, ALU.max),
            reads=(pb(3),), writes=("sq5",))
        P.do("vector", lambda v: v.tensor_single_scalar(
            C.srow[:, 6:7], C.srow[:, 5:6], EPS, ALU.max), reads=("sq5",), writes=("sq6",))
        P.do("tensor", lambda t, sl=sl: t.transpose(
            C.PB[0:1, 1568:1696], C.srow[:, 6:7], C.eye[:, 0:128]),
            reads=("sq6", "eye"), writes=(pb(3),))
        P.do("vector", lambda v, sl=sl, row_off=row_off: v.tensor_copy(
            C.row[0][0:1, row_off + 128 * sl:row_off + 128 * (sl + 1)],
            C.PB[0:1, 1568:1696]), reads=(pb(3),), writes=("row0",))
    ro = slice(row_off, row_off + 128 * nslices)
    def _recip(v):
        with C.nc.allow_low_precision(reason="bf16 act-quant scale"):
            return v.reciprocal(C.row[1][0:1, ro], C.row[0][0:1, ro])
    P.do("vector", _recip, reads=("row0",), writes=("row1",))
    P.do("vector", lambda v: v.tensor_single_scalar(
        C.row[1][0:1, ro], C.row[1][0:1, ro], 127.0, ALU.mult),
        reads=("row1",), writes=("row1",))
    P.do("vector", lambda v: v.tensor_single_scalar(
        C.row[0][0:1, ro], C.row[0][0:1, ro], -1.0 / 127.0, ALU.mult),
        reads=("row0",), writes=("row0",))


def bcast_rows(core, w0, row_off):
    P, C = core.P, core
    P.do("tensor", lambda t: t.matmul(
        C.PB[:, 1024:1536], C.onesrb[0:1, :], C.row[1][0:1, row_off + w0:row_off + w0 + 512],
        start=True, stop=True), reads=("row1", "onesrb"), writes=(pb(2),))
    P.do("vector", lambda v: v.tensor_copy(C.bc[0][:], C.PB[:, 1024:1536]),
         reads=(pb(2),), writes=("bc0",))
    P.do("tensor", lambda t: t.matmul(
        C.PB[:, 1024:1536], C.onesrb[0:1, :], C.row[0][0:1, row_off + w0:row_off + w0 + 512],
        start=True, stop=True), reads=("row0", "onesrb"), writes=(pb(2),))
    P.do("vector", lambda v: v.tensor_copy(C.bc[1][:], C.PB[:, 1024:1536]),
         reads=(pb(2),), writes=("bc1",))


def stage_l1_ffn(core, W):
    P, C = core.P, core
    ln(core)
    quant_global(core, W["L1_ffn_w_gate"], FF, 1024, W["scrW"], "SCRW")
    quant_global(core, W["L1_ffn_w_up"], FF, 1024, W["scrW3"], "SCRW3")
    quant_global(core, W["L1_ffn_w_down"], 1024, FF, W["scrW2"], "SCRW2")
    P.do("vector", lambda v: v.tensor_copy(C.B[0][:], C.A[0][:]),
         reads=("A0",), writes=("B0",))
    P.do("gpsimd", lambda g: g.tensor_copy(C.B[1][:], C.A[0][:]),
         reads=("A0",), writes=("B1",))
    for c in range(1, 8):
        P.do("vector", lambda v, c=c: v.tensor_tensor(C.B[0][:], C.B[0][:], C.A[c][:],
                                                      ALU.max),
             reads=("B0", f"A{c}"), writes=("B0",))
        P.do("vector", lambda v, c=c: v.tensor_tensor(C.B[1][:], C.B[1][:], C.A[c][:],
                                                      ALU.min),
             reads=("B1", f"A{c}"), writes=("B1",))
    for (buf, bkey, op) in ((C.B[0], "B0", ALU.max), (C.B[1], "B1", ALU.min)):
        for (n0, n1) in ((64, 64), (32, 32)):
            P.do("vector", lambda v, buf=buf, n0=n0: v.tensor_copy(
                C.PB[0:n0, 0:S], buf[n0:2 * n0, :]),
                reads=(bkey,), writes=PBALL)
            P.do("vector", lambda v, buf=buf, op=op, n0=n0: v.tensor_tensor(
                buf[0:n0, :], buf[0:n0, :], C.PB[0:n0, 0:S], op),
                reads=(bkey,) + PBALL, writes=(bkey,))
    P.do("vector", lambda v: v.tensor_single_scalar(
        C.B[1][0:32, :], C.B[1][0:32, :], -1.0, ALU.mult), reads=("B1",), writes=("B1",))
    P.do("vector", lambda v: v.tensor_tensor(
        C.B[0][0:32, :], C.B[0][0:32, :], C.B[1][0:32, :], ALU.max),
        reads=("B0", "B1"), writes=("B0",))
    for half in range(2):
        P.do("vector", lambda v, half=half: v.tensor_copy(
            C.SCR4[0:32, 0:1024], C.B[0][0:32, 1024 * half:1024 * (half + 1)]),
            reads=("B0",), writes=("SCR4a", "SCR4b"))
        amax_cols_to_scales(core, C.SCR4[0:32, 0:1024], ("SCR4a", "SCR4b"), 8, 1024 * half)
    for w in range(4):
        bcast_rows(core, 512 * w, 0)
        for c in range(8):
            sl = slice(512 * w, 512 * (w + 1))
            act_quant_tile(core, C.A[c][:, sl], (f"A{c}",), C.A[c][:, sl], f"A{c}", w)
    for tt in range(4):
        ts = slice(512 * tt, 512 * (tt + 1))
        for g in range(4):
            for i in range(8):
                P.do("sync", lambda s, g=g, i=i: s.dma_start(
                    out=C.VT[i][:, 0:1024], in_=W["scrW"][1024 * g:1024 * (g + 1),
                                                          128 * i:128 * (i + 1)],
                    transpose=True), reads=("SCRW",), writes=(f"VT{i}",))
                P.do("sync", lambda s, g=g, i=i: s.dma_start(
                    out=C.VT[8 + i][:, 0:1024], in_=W["scrW3"][1024 * g:1024 * (g + 1),
                                                              128 * i:128 * (i + 1)],
                    transpose=True), reads=("SCRW3",), writes=(f"VT{8 + i}",))
            for j in range(8):
                jj = 8 * g + j
                bk_g = 2 * (jj % 2)
                bk_u = 2 * (jj % 2) + 1
                for i in range(8):
                    P.do("tensor", lambda t, i=i, j=j, bk_g=bk_g, ts=ts: t.matmul(
                        C.PB[:, 512 * bk_g:512 * (bk_g + 1)],
                        C.VT[i][:, 128 * j:128 * (j + 1)], C.A[i][:, ts],
                        start=(i == 0), stop=(i == 7)), reads=(f"VT{i}", f"A{i}"),
                        writes=(pb(bk_g),))
                    P.do("tensor", lambda t, i=i, j=j, bk_u=bk_u, ts=ts: t.matmul(
                        C.PB[:, 512 * bk_u:512 * (bk_u + 1)],
                        C.VT[8 + i][:, 128 * j:128 * (j + 1)], C.A[i][:, ts],
                        start=(i == 0), stop=(i == 7)), reads=(f"VT{8 + i}", f"A{i}"),
                        writes=(pb(bk_u),))
                P.do("scalar", lambda s, bk_g=bk_g: s.activation(
                    C.FS[0], C.PB[:, 512 * bk_g:512 * (bk_g + 1)], AF.Silu),
                    reads=(pb(bk_g),), writes=("SCR4a", "SCR4b"))
                P.do("vector", lambda v, jj=jj, bk_u=bk_u: v.tensor_tensor(
                    C.B[jj // 4][:, 512 * (jj % 4):512 * (jj % 4 + 1)], C.FS[0],
                    C.PB[:, 512 * bk_u:512 * (bk_u + 1)], ALU.mult),
                    reads=("SCR4a", pb(bk_u)), writes=(f"B{jj // 4}",))
        for jj in range(32):
            gt = C.B[jj // 4][:, 512 * (jj % 4):512 * (jj % 4 + 1)]
            if jj == 0:
                P.do("vector", lambda v, gt=gt: v.tensor_copy(C.FS[0], gt),
                     reads=(f"B{jj // 4}",), writes=("SCR4a",))
                P.do("gpsimd", lambda g, gt=gt: g.tensor_copy(C.FS[1], gt),
                     reads=(f"B{jj // 4}",), writes=("SCR4b",))
            else:
                P.do("vector", lambda v, gt=gt, jj=jj: v.tensor_tensor(
                    C.FS[0], C.FS[0], gt, ALU.max),
                    reads=("SCR4a", f"B{jj // 4}"), writes=("SCR4a",))
                P.do("vector", lambda v, gt=gt, jj=jj: v.tensor_tensor(
                    C.FS[1], C.FS[1], gt, ALU.min),
                    reads=("SCR4b", f"B{jj // 4}"), writes=("SCR4b",))
        for n0 in (64, 32):
            P.do("vector", lambda v, n0=n0: v.tensor_copy(
                C.PB[0:n0, 1536:2048], C.SCR4[n0:2 * n0, 0:512]),
                reads=("SCR4a",), writes=(pb(3),))
            P.do("vector", lambda v, n0=n0: v.tensor_tensor(
                C.SCR4[0:n0, 0:512], C.SCR4[0:n0, 0:512], C.PB[0:n0, 1536:2048],
                ALU.max), reads=("SCR4a",) + (pb(3),), writes=("SCR4a",))
        for n0 in (64, 32):
            P.do("vector", lambda v, n0=n0: v.tensor_copy(
                C.PB[0:n0, 1536:2048], C.FS[1][n0:2 * n0, :]),
                reads=("SCR4b",), writes=(pb(3),))
            P.do("vector", lambda v, n0=n0: v.tensor_tensor(
                C.FS[1][0:n0, :], C.FS[1][0:n0, :], C.PB[0:n0, 1536:2048],
                ALU.min), reads=("SCR4b",) + (pb(3),), writes=("SCR4b",))
        P.do("vector", lambda v: v.tensor_single_scalar(
            C.FS[1][0:32, :], C.FS[1][0:32, :], -1.0, ALU.mult),
            reads=("SCR4b",), writes=("SCR4b",))
        P.do("vector", lambda v: v.tensor_tensor(
            C.SCR4[0:32, 0:512], C.SCR4[0:32, 0:512], C.FS[1][0:32, :], ALU.max),
            reads=("SCR4a", "SCR4b"), writes=("SCR4a",))
        amax_cols_to_scales(core, C.SCR4[0:32, 0:512], ("SCR4a",), 4, 0)
        bcast_rows(core, 0, 0)
        for jj in range(32):
            gt = C.B[jj // 4][:, 512 * (jj % 4):512 * (jj % 4 + 1)]
            act_quant_tile(core, gt, (f"B{jj // 4}",), gt, f"B{jj // 4}", 0)
        for jj in range(32):
            ph = C.POOL[5][:, 1024 * (jj % 2):1024 * (jj % 2 + 1)]
            pkey = "PL5a" if jj % 2 == 0 else "PL5b"
            P.do("sync", lambda s, jj=jj, ph=ph: s.dma_start(
                out=ph, in_=W["scrW2"][:, 128 * jj:128 * (jj + 1)], transpose=True),
                reads=("SCRW2",), writes=(pkey,))
            for o in range(8):
                pbank = C.PB[:, 512 * (o % 4):512 * (o % 4 + 1)] if o < 4 else \
                    C.PC[:, 512 * (o % 4):512 * (o % 4 + 1)]
                bkey = pb(o % 4) if o < 4 else pc(o % 4)
                P.do("tensor", lambda t, jj=jj, o=o, ph=ph, pbank=pbank: t.matmul(
                    pbank, ph[:, 128 * o:128 * (o + 1)],
                    C.B[jj // 4][:, 512 * (jj % 4):512 * (jj % 4 + 1)],
                    start=(jj == 0), stop=(jj == 31)),
                    reads=(pkey, f"B{jj // 4}"), writes=(bkey,))
        for o in range(8):
            pbank = C.PB[:, 512 * (o % 4):512 * (o % 4 + 1)] if o < 4 else \
                C.PC[:, 512 * (o % 4):512 * (o % 4 + 1)]
            bkey = pb(o % 4) if o < 4 else pc(o % 4)
            P.do("vector", lambda v, o=o, ts=ts, pbank=pbank: v.tensor_tensor(
                C.X[o][:, ts], pbank, C.X[o][:, ts], ALU.add),
                reads=(bkey, f"X{o}"), writes=(f"X{o}",))



def _make_consts():
    k = np.arange(128)[:, None]
    q = np.arange(384)[None, :]
    mask = np.where(np.abs(q - 128 - k) <= 128, 0.0, NEG).astype(np.float32)
    ones = np.ones((128, 1), np.float32)
    onesr = np.ones((1, 128), np.float32)
    return {"c_eye": np.eye(128, dtype=np.float32), "c_ones": ones,
            "c_onesb": ones.astype(BF), "c_onesr": onesr,
            "c_onesrb": onesr.astype(BF), "c_mask": mask.astype(BF)}


_CONST_SPECS = [("c_eye", (128, 128), F32), ("c_ones", (128, 1), F32),
                ("c_onesb", (128, 1), BF16), ("c_onesr", (1, 128), F32),
                ("c_onesrb", (1, 128), BF16), ("c_mask", (128, 384), BF16)]

_STAGE_WEIGHTS = {
    "l0a": ["L0_attn_Wq", "L0_attn_Wk", "L0_attn_Wv", "L0_attn_Wo",
            "L0_attn_bq", "L0_attn_bk", "L0_attn_bv", "L0_attn_bo"],
    "l0f": ["L0_ffn_fast_b", "L0_ffn_w_fc", "L0_ffn_w_proj", "L0_ffn_u",
            "L0_ffn_v", "L0_ffn_gate"],
    "l1a": ["L1_attn_q_w", "L1_attn_q_b", "L1_attn_k_w", "L1_attn_k_b",
            "L1_attn_v_w", "L1_attn_v_b", "L1_attn_l_w", "L1_attn_l_b"],
    "l1f": ["L1_ffn_w_gate", "L1_ffn_w_up", "L1_ffn_w_down"],
}

_W_SHAPES = {
    "L0_attn_Wq": (D, D), "L0_attn_Wk": (D, D), "L0_attn_Wv": (D, D),
    "L0_attn_Wo": (D, D),
    "L0_attn_bq": (D, 1), "L0_attn_bk": (D, 1), "L0_attn_bv": (D, 1),
    "L0_attn_bo": (D, 1),
    "L0_ffn_fast_b": (16, 16), "L0_ffn_w_fc": (FF, D), "L0_ffn_w_proj": (D, FF),
    "L0_ffn_u": (FF, 16), "L0_ffn_v": (16, D), "L0_ffn_gate": (1, 1),
    "L1_attn_q_w": (D, D), "L1_attn_k_w": (D, D), "L1_attn_v_w": (D, D),
    "L1_attn_l_w": (D, D),
    "L1_attn_q_b": (D, 1), "L1_attn_k_b": (D, 1), "L1_attn_v_b": (D, 1),
    "L1_attn_l_b": (D, 1),
    "L1_ffn_w_gate": (FF, D), "L1_ffn_w_up": (FF, D), "L1_ffn_w_down": (D, FF),
}

_STAGE_FN = {"l0a": stage_l0_attn, "l0f": stage_l0_ffn, "l1a": stage_l1_attn,
             "l1f": stage_l1_ffn}


def _host_weights(inputs):
    w = {}
    for k, v in inputs.items():
        if k == "L0_attn_x":
            continue
        a = np.asarray(v)
        if k in ("L1_attn_q_w", "L1_attn_k_w", "L1_attn_v_w"):
            a = a.reshape(D, D)
        if k in _W_SHAPES:
            a = a.reshape(_W_SHAPES[k])
        w[k] = np.ascontiguousarray(a, dtype=np.float32)
    return w


def _build_program(stages):
    nc = bass.Bass("TRN2", target_bir_lowering=False)
    dram = {}
    names_in = ["x_in"]
    dram["x_in"] = nc.dram_tensor("x_in", [D, S], F32, kind="ExternalInput").ap()
    for cn, sh, dt in _CONST_SPECS:
        dram[cn] = nc.dram_tensor(cn, list(sh), dt, kind="ExternalInput").ap()
        names_in.append(cn)
    for st in stages:
        for n in _STAGE_WEIGHTS[st]:
            dram[n] = nc.dram_tensor(n, list(_W_SHAPES[n]), F32, kind="ExternalInput").ap()
            names_in.append(n)
    if "l0a" in stages or "l1a" in stages:
        dram["scrV"] = nc.dram_tensor("scrV", [D, S], BF16, kind="Internal").ap()
    if "l0f" in stages or "l1f" in stages or "l1a" in stages:
        dram["scrW"] = nc.dram_tensor("scrW", [FF, D], BF16, kind="Internal").ap()
        dram["scrW2"] = nc.dram_tensor("scrW2", [D, FF], BF16, kind="Internal").ap()
        dram["scrSM"] = nc.dram_tensor("scrSM", [32, 1024], BF16, kind="Internal").ap()
        dram["scrU"] = nc.dram_tensor("scrU", [FF, 128], BF16, kind="Internal").ap()
        dram["scrW3"] = nc.dram_tensor("scrW3", [FF, D], BF16, kind="Internal").ap()
    x_out = nc.dram_tensor("x_out", [D, S], F32, kind="ExternalOutput").ap()

    with ExitStack() as ctx:
        core = Core(nc, ctx)
        C, P = core, core.P
        for cn, sh, dt in _CONST_SPECS:
            C.dma(getattr(C, {"c_eye": "eye", "c_ones": "ones", "c_onesb": "onesb",
                              "c_onesr": "onesr", "c_onesrb": "onesrb",
                              "c_mask": "mask"}[cn])[:], dram[cn], cn)
        for c in range(8):
            C.dma(C.X[c][:], dram["x_in"][128 * c:128 * (c + 1), :], f"X{c}")
        snap_outs = {}
        for st in stages:
            _STAGE_FN[st](core, dram)
            if _SNAP_X:
                so = nc.dram_tensor("snap_" + st, [D, S], F32, kind="ExternalOutput").ap()
                snap_outs[st] = so
                for c in range(8):
                    P.do("sync", lambda s, c=c, so=so: s.dma_start(
                        out=so[128 * c:128 * (c + 1), :], in_=C.X[c][:]),
                        reads=(f"X{c}",), writes=(f"SNP{st}{c}",))
        tap_names = []
        for (tn, tfn, tsh, tdt, tkeys) in _DEBUG_TAPS:
            tout = nc.dram_tensor("tap_" + tn, list(tsh), tdt, kind="ExternalOutput").ap()
            P.do("sync", lambda s, tout=tout, tfn=tfn: s.dma_start(out=tout, in_=tfn(core)),
                 reads=tuple(tkeys), writes=("TAP" + tn,))
            tap_names.append(tn)
        for c in range(8):
            P.do("sync", lambda s, c=c: s.dma_start(
                out=x_out[128 * c:128 * (c + 1), :], in_=C.X[c][:]),
                reads=(f"X{c}",), writes=(f"XO{c}",))
        P.barrier("sync", [f"XO{c}" for c in range(8)] + ["TAP" + t for t in tap_names])
        with nc.Block() as blk:
            P.emit(blk, C.sems)
    return nc, names_in


def _run(stages, xs, w, consts):
    nc, names_in = _build_program(stages)
    in_maps = []
    for b in range(8):
        m = {"x_in": xs[b]}
        for k in names_in:
            if k.startswith("c_"):
                m[k] = consts[k]
            elif k != "x_in":
                m[k] = w[k]
        in_maps.append(m)
    res = run_bass_kernel_spmd(nc, in_maps, core_ids=list(range(8)))
    global _TAP_RESULTS
    _TAP_RESULTS = [{k: np.asarray(v) for k, v in r.items() if k.startswith("tap_")}
                    for r in res.results]
    return [np.asarray(r["x_out"]) for r in res.results]


def kernel(**inputs):
    consts = _make_consts()
    w = _host_weights(inputs)
    xs = [np.ascontiguousarray(np.asarray(inputs["L0_attn_x"][b], np.float32).T)
          for b in range(8)]
    if _FUSED:
        launches = [list(_STAGES)]
    else:
        launches = [[st] for st in _STAGES]
    for lst in launches:
        xs = _run(lst, xs, w, consts)
    return np.stack([x.T for x in xs]).astype(np.float32)
